# version 05
# baseline (speedup 1.0000x reference)
import time
import numpy as np
from contextlib import ExitStack

import concourse.bass as bass
import concourse.mybir as mybir
from concourse.bass_utils import run_bass_kernel_spmd

F32 = mybir.dt.float32
BF16 = mybir.dt.bfloat16
I32 = mybir.dt.int32
AF = mybir.ActivationFunctionType
ALU = mybir.AluOpType
AX = mybir.AxisListType

D = 2048
NDC = 16
TOK = 1024
NTC = 8
E = 8
HID = 1408
NHC = 11
NH = 512
EPS_LN = 1e-5
EPS_RMS = float(np.finfo(np.float32).eps)


class Ctr:
    __slots__ = ("h", "n")

    def __init__(self, h):
        self.h = h
        self.n = 0


class B:
    pass


def _mk_ctr(bx, name):
    c = Ctr(bx.es.enter_context(bx.nc.semaphore(name)))
    bx.ctrs.append(c)
    return c


def _dma(bx, eng, out, in_, ctr):
    eng.dma_start(out=out, in_=in_).then_inc(ctr.h, 16)
    ctr.n += 16
    return ctr.n


def _inc(instr, ctr, k=1):
    instr.then_inc(ctr.h, k)
    ctr.n += k
    return ctr.n


def _engines(nc):
    return (nc.sync, nc.tensor, nc.vector, nc.scalar, nc.gpsimd)


def _barrier(bx):
    for eng in _engines(bx.nc):
        for c in bx.ctrs:
            if c.n:
                eng.wait_ge(c.h, c.n)


def _join(bx, engs):
    for eng in engs:
        for c in bx.ctrs:
            if c.n:
                eng.wait_ge(c.h, c.n)


def emit_ln(bx, with_rms, tag, gate=False):
    nc = bx.nc
    v, sc, g, t = nc.vector, nc.scalar, nc.gpsimd, nc.tensor
    ps = bx.ps
    cT, cV, cS, cG = bx.cMM, bx.cDV, bx.cSC, bx.cGP

    es = ExitStack()
    if gate:
        lnf = es.enter_context(nc.sbuf_tensor(f"{tag}_lnf", [128, 2048], F32))
    scrb = es.enter_context(nc.sbuf_tensor(f"{tag}_xbf", [128, 2048], BF16))
    sqb = es.enter_context(nc.sbuf_tensor(f"{tag}_sqb", [128, 2048], BF16))
    stats = es.enter_context(nc.sbuf_tensor(f"{tag}_stats", [1, 4096], F32))
    A_b = es.enter_context(nc.sbuf_tensor(f"{tag}_Ab", [128, 1024], F32))
    B_b = es.enter_context(nc.sbuf_tensor(f"{tag}_Bb", [128, 1024], F32))
    mean_r = stats[:, 0:1024]
    var_r = stats[:, 1024:2048]
    s_r = stats[:, 2048:3072]
    t_r = stats[:, 3072:4096]

    entry_cV = cV.n
    entry_cS = cS.n
    cast_done = []
    sq_done = []
    mm_done = []
    for dc in range(NDC):
        sl = dc % 2
        xb = scrb[:, sl * 1024:(sl + 1) * 1024]
        sq = sqb[:, sl * 1024:(sl + 1) * 1024]
        if dc >= 2:
            g.wait_ge(cT.h, mm_done[dc - 2])
        i = g.tensor_copy(out=xb, in_=bx.x_res[:, dc * 1024:(dc + 1) * 1024])
        _inc(i, cG)
        cast_done.append(cG.n)
        sc.wait_ge(cG.h, cG.n)
        i = sc.activation(out=sq, in_=xb, func=AF.Square)
        _inc(i, cS)
        sq_done.append(cS.n)
        t.wait_ge(cG.h, cast_done[dc])
        t.wait_ge(cS.h, sq_done[dc])
        if dc == 0:
            if entry_cV:
                t.wait_ge(cV.h, entry_cV)
            if entry_cS:
                t.wait_ge(cS.h, entry_cS)
        st = (dc == 0)
        sp = (dc == NDC - 1)
        for h in range(2):
            t.matmul(out=ps[0 + h][0:1, :], lhsT=bx.ones_lhs[:, 0:1],
                     rhs=xb[:, h * NH:(h + 1) * NH], start=st, stop=sp)
            i = t.matmul(out=ps[2 + h][0:1, :], lhsT=bx.ones_lhs[:, 0:1],
                         rhs=sq[:, h * NH:(h + 1) * NH], start=st, stop=sp)
        _inc(i, cT)
        mm_done.append(cT.n)

    v.wait_ge(cT.h, cT.n)
    v.tensor_copy(out=mean_r[:, 0:512], in_=ps[0][0:1, :])
    v.tensor_copy(out=mean_r[:, 512:1024], in_=ps[1][0:1, :])
    v.tensor_copy(out=var_r[:, 0:512], in_=ps[2][0:1, :])
    i = v.tensor_copy(out=var_r[:, 512:1024], in_=ps[3][0:1, :])
    _inc(i, cV)

    def vstep(fn):
        v.wait_ge(cV.h, cV.n)
        _inc(fn(), cV)

    def scstep(fn):
        sc.wait_ge(cV.h, cV.n)
        _inc(fn(), cS)
        v.wait_ge(cS.h, cS.n)

    vstep(lambda: v.tensor_mul(out=t_r, in0=mean_r, in1=mean_r))
    vstep(lambda: v.tensor_sub(out=var_r, in0=var_r, in1=t_r))
    scstep(lambda: sc.activation(out=t_r, in_=var_r, func=AF.Sqrt,
                                 bias=bx.consts[0:1, 0:1]))
    vstep(lambda: v.reciprocal(out=s_r, in_=t_r))
    if with_rms:
        vstep(lambda: v.tensor_mul(out=t_r, in0=s_r, in1=s_r))
        vstep(lambda: v.tensor_mul(out=t_r, in0=var_r, in1=t_r))
        scstep(lambda: sc.activation(out=t_r, in_=t_r, func=AF.Sqrt,
                                     bias=bx.consts[0:1, 1:2]))
        vstep(lambda: v.reciprocal(out=t_r, in_=t_r))
        vstep(lambda: v.tensor_mul(out=s_r, in0=s_r, in1=t_r))
    vstep(lambda: v.tensor_mul(out=t_r, in0=mean_r, in1=s_r))

    t.wait_ge(cV.h, cV.n)
    for h in range(2):
        t.matmul(out=ps[0 + h][:, :], lhsT=bx.ones_f1[0:1, :],
                 rhs=s_r[0:1, h * NH:(h + 1) * NH], start=True, stop=True)
        i = t.matmul(out=ps[2 + h][:, :], lhsT=bx.ones_f1[0:1, :],
                     rhs=t_r[0:1, h * NH:(h + 1) * NH], start=True, stop=True)
    _inc(i, cT)
    v.wait_ge(cT.h, cT.n)
    v.tensor_copy(out=A_b[:, 0:512], in_=ps[0][:, :])
    v.tensor_copy(out=A_b[:, 512:1024], in_=ps[1][:, :])
    v.tensor_copy(out=B_b[:, 0:512], in_=ps[2][:, :])
    i = v.tensor_copy(out=B_b[:, 512:1024], in_=ps[3][:, :])
    _inc(i, cV)
    bcast_v = cV.n

    apply_v = []
    lnf_users = []
    for dc in range(NDC):
        sl = dc % 2
        tslot = bx.scr[:, sl * 1024:(sl + 1) * 1024]
        g.wait_ge(cV.h, bcast_v)
        if dc >= 2:
            g.wait_ge(cV.h, apply_v[dc - 2])
        i = g.tensor_mul(out=tslot, in0=bx.x_res[:, dc * 1024:(dc + 1) * 1024], in1=A_b[:, :])
        _inc(i, cG)
        v.wait_ge(cG.h, cG.n)
        if not gate:
            i = v.tensor_sub(out=bx.ln[:, dc * 1024:(dc + 1) * 1024], in0=tslot, in1=B_b[:, :])
            _inc(i, cV)
            apply_v.append(cV.n)
        else:
            lslot = lnf[:, sl * 1024:(sl + 1) * 1024]
            if dc >= 2:
                v.wait_ge(cT.h, lnf_users[dc - 2][0])
                v.wait_ge(cS.h, lnf_users[dc - 2][1])
            i = v.tensor_sub(out=lslot, in0=tslot, in1=B_b[:, :])
            _inc(i, cV)
            apply_v.append(cV.n)
            sc.wait_ge(cV.h, cV.n)
            i = sc.activation(out=bx.ln[:, dc * 1024:(dc + 1) * 1024], in_=lslot, func=AF.Copy)
            _inc(i, cS)
            t.wait_ge(cV.h, cV.n)
            for h in range(2):
                i = t.matmul(out=ps[4 + h][0:8, :], lhsT=bx.wgf_f[:, dc * 8:(dc + 1) * 8],
                             rhs=lslot[:, h * NH:(h + 1) * NH],
                             start=(dc == 0), stop=(dc == NDC - 1))
            _inc(i, cT)
            lnf_users.append((cT.n, cS.n))
    es.close()
    if gate:
        return cS.n, cT.n
    return cV.n


class Streamer:

    def __init__(self, bx):
        self.bx = bx
        self.idx = 0
        self.cast_hist = []

    def stream(self, src_ap, dst_bf, free_at_cT):
        bx = self.bx
        nc = bx.nc
        sy, sc = nc.sync, nc.scalar
        slot = self.idx % 2
        self.idx += 1
        F = dst_bf.shape[-1]
        stg = bx.stg[:, slot * 2048: slot * 2048 + F]
        if len(self.cast_hist) >= 2:
            sy.wait_ge(bx.cSC.h, self.cast_hist[-2])
        cws = bx.cWs[slot]
        _dma(bx, sy, stg.rearrange("p (c m) -> p c m", m=128), src_ap, cws)
        sc.wait_ge(cws.h, cws.n)
        if free_at_cT is not None:
            sc.wait_ge(bx.cMM.h, free_at_cT)
        i = sc.activation(out=dst_bf, in_=stg, func=AF.Copy)
        _inc(i, bx.cSC)
        self.cast_hist.append(bx.cSC.n)
        return bx.cSC.n


def emit_ffn(bx, w1_in, w2_in, w3_in, tag):
    nc = bx.nc
    v, sc, g, t = nc.vector, nc.scalar, nc.gpsimd, nc.tensor
    ps = bx.ps
    cT, cV, cS, cG = bx.cMM, bx.cDV, bx.cSC, bx.cGP

    es0 = ExitStack()
    bx.scr = es0.enter_context(nc.sbuf_tensor(f"{tag}_scr", [128, 2048], F32))
    bx.stg = es0.enter_context(nc.sbuf_tensor(f"{tag}_stg", [128, 2 * 2048], F32))

    ln_s, gate_t = emit_ln(bx, with_rms=False, tag=tag + "ln", gate=True)
    ln_vlast = cV.n
    _join(bx, (nc.sync, nc.vector, nc.scalar, nc.gpsimd))

    es = ExitStack()
    sc_f = es.enter_context(nc.sbuf_tensor(f"{tag}_scf", [32, 1024], F32))
    tv = es.enter_context(nc.sbuf_tensor(f"{tag}_tv", [32, 1024], F32))
    eq1 = es.enter_context(nc.sbuf_tensor(f"{tag}_eq1", [32, 256], F32))
    eq2 = es.enter_context(nc.sbuf_tensor(f"{tag}_eq2", [32, 256], F32))
    scm = es.enter_context(nc.sbuf_tensor(f"{tag}_scm", [32, 256], F32))
    mx1 = es.enter_context(nc.sbuf_tensor(f"{tag}_mx1", [32, 32], F32))
    mx2 = es.enter_context(nc.sbuf_tensor(f"{tag}_mx2", [32, 32], F32))
    p1h = es.enter_context(nc.sbuf_tensor(f"{tag}_p1h", [32, 32], F32))
    p2h = es.enter_context(nc.sbuf_tensor(f"{tag}_p2h", [32, 32], F32))
    gw = es.enter_context(nc.sbuf_tensor(f"{tag}_gw", [32, 1024], F32))
    w_f = es.enter_context(nc.sbuf_tensor(f"{tag}_wf", [32, 1024], F32))

    def vstep(fn):
        v.wait_ge(cV.h, cV.n)
        _inc(fn(), cV)

    v.wait_ge(cT.h, gate_t)
    _inc(v.memset(sc_f[:, :], 0.0), cV)
    _inc(v.memset(gw[:, :], 0.0), cV)
    vstep(lambda: v.tensor_copy(out=sc_f[0:8, 0:512], in_=ps[4][0:8, :]))
    vstep(lambda: v.tensor_copy(out=sc_f[0:8, 512:1024], in_=ps[5][0:8, :]))
    vstep(lambda: v.transpose(out=tv[:, :], in_=sc_f[:, :]))
    tvv = tv[:, :].rearrange("p (cb e) -> p cb e", e=32)[:, :, 0:8]
    eq1v = eq1[:, :].rearrange("p (cb e) -> p cb e", e=8)
    eq2v = eq2[:, :].rearrange("p (cb e) -> p cb e", e=8)
    scmv = scm[:, :].rearrange("p (cb e) -> p cb e", e=8)
    gwv = gw[:, :].rearrange("p (cb e) -> p cb e", e=32)[:, :, 0:8]
    mx1v = mx1[:, :].rearrange("p (cb o) -> p cb o", o=1)
    mx2v = mx2[:, :].rearrange("p (cb o) -> p cb o", o=1)
    p1v = p1h[:, :].rearrange("p (cb o) -> p cb o", o=1)
    p2v = p2h[:, :].rearrange("p (cb o) -> p cb o", o=1)
    vstep(lambda: v.reduce_max(out=mx1v, in_=tvv, axis=AX.X))
    vstep(lambda: v.tensor_tensor(out=eq1v, in0=tvv, in1=mx1v.broadcast_to([32, 32, 8]),
                                  op=ALU.is_equal))
    vstep(lambda: v.scalar_tensor_tensor(out=scmv, in0=eq1v, scalar=-1e30, in1=tvv,
                                         op0=ALU.mult, op1=ALU.add))
    vstep(lambda: v.reduce_max(out=mx2v, in_=scmv, axis=AX.X))
    vstep(lambda: v.tensor_tensor(out=eq2v, in0=scmv, in1=mx2v.broadcast_to([32, 32, 8]),
                                  op=ALU.is_equal))
    vstep(lambda: v.tensor_sub(out=p2h[:, :], in0=mx1[:, :], in1=mx2[:, :]))
    sc.wait_ge(cV.h, cV.n)
    _inc(sc.activation(out=p1h[:, :], in_=p2h[:, :], func=AF.Sigmoid), bx.cSC)
    v.wait_ge(bx.cSC.h, bx.cSC.n)
    _inc(v.tensor_scalar(out=p1h[:, :], in0=p1h[:, :], scalar1=0.5, scalar2=None,
                         op0=ALU.mult), cV)
    vstep(lambda: v.tensor_scalar(out=p2h[:, :], in0=p1h[:, :], scalar1=-1.0, scalar2=0.5,
                                  op0=ALU.mult, op1=ALU.add))
    vstep(lambda: v.tensor_tensor(out=gwv, in0=eq1v, in1=p1v.broadcast_to([32, 32, 8]),
                                  op=ALU.mult))
    vstep(lambda: v.tensor_tensor(out=eq2v, in0=eq2v, in1=p2v.broadcast_to([32, 32, 8]),
                                  op=ALU.mult))
    vstep(lambda: v.tensor_tensor(out=gwv, in0=gwv, in1=eq2v, op=ALU.add))
    vstep(lambda: v.transpose(out=w_f[:, :], in_=gw[:, :]))
    wf_v = cV.n

    es2 = ExitStack()
    hbuf = es2.enter_context(nc.sbuf_tensor(f"{tag}_h", [128, NHC * 1024], BF16))
    sel_sb = es2.enter_context(nc.sbuf_tensor(f"{tag}_sel", [8, 1024], F32))
    _dma(bx, nc.sync, sel_sb[:, :], bx.sel_in, bx.cX)
    w1bf = es2.enter_context(nc.sbuf_tensor(f"{tag}_w1b", [128, 2 * 2048], BF16))
    w2bf = es2.enter_context(nc.sbuf_tensor(f"{tag}_w2b", [128, 2 * 2048], BF16))
    w3bf = es2.enter_context(nc.sbuf_tensor(f"{tag}_w3b", [128, 2 * 1408], BF16))
    wb = es2.enter_context(nc.sbuf_tensor(f"{tag}_wb", [128, 1024], F32))

    strm = Streamer(bx)
    h_mm = {}
    y_mm = {}
    h_ev = {}
    y_ev = {}

    def w1_src(e, hc):
        return w1_in[e].rearrange("(dc p) m -> p dc m", p=128)[:, :, hc * 128:(hc + 1) * 128]

    def w2_src(e, hc):
        return w2_in[e].rearrange("(dc p) m -> p dc m", p=128)[:, :, hc * 128:(hc + 1) * 128]

    def w3_src(e, dc):
        return w3_in[e].rearrange("(hc p) m -> p hc m", p=128)[:, :, dc * 128:(dc + 1) * 128]

    sched = []
    for e in range(E):
        for hc in range(NHC):
            sched.append(("w1", e, hc))
            sched.append(("w2", e, hc))
        for dc in range(NDC):
            sched.append(("w3", e, dc))
    cast_of = {}
    sidx = {ent: i for i, ent in enumerate(sched)}
    ptr = [0]

    def _freeat(ent):
        kind, e, j = ent
        if kind in ("w1", "w2"):
            pk = (e, j - 2) if j >= 2 else ((e - 1, NHC - 2 + j) if e > 0 else None)
            return (pk is None, h_mm.get(pk + (1,)) if pk is not None else None)
        pk = (e, j - 2) if j >= 2 else ((e - 1, NDC - 2 + j) if e > 0 else None)
        return (pk is None, y_mm.get(pk))

    def _emit(ent):
        kind, e, j = ent
        sl = j % 2
        fresh, fa = _freeat(ent)
        if kind == "w1":
            cast_of[ent] = strm.stream(w1_src(e, j), w1bf[:, sl * 2048:(sl + 1) * 2048], fa)
        elif kind == "w2":
            cast_of[ent] = strm.stream(w2_src(e, j), w2bf[:, sl * 2048:(sl + 1) * 2048], fa)
        else:
            cast_of[ent] = strm.stream(w3_src(e, j), w3bf[:, sl * 1408:(sl + 1) * 1408], fa)

    def pump(ent, ahead=3):
        target = sidx[ent] + ahead
        while ptr[0] < len(sched) and (sched[ptr[0]] in (ent,) or ptr[0] <= target):
            nxt = sched[ptr[0]]
            if nxt != ent and sidx[nxt] > sidx[ent]:
                fresh, fa = _freeat(nxt)
                if not fresh and fa is None:
                    break
            _emit(nxt)
            ptr[0] += 1
            if nxt == ent:
                if ptr[0] > target:
                    break

    wb_prev_v = [None]

    def emit_wb(e):
        t.wait_ge(cV.h, wf_v)
        t.wait_ge(bx.cX.h, bx.cX.n)
        if wb_prev_v[0] is not None:
            t.wait_ge(cV.h, wb_prev_v[0])
        for h in range(2):
            if h == 1:
                t.wait_ge(cV.h, cV.n)
            i = t.matmul(out=ps[6][:, :], lhsT=sel_sb[:, e * 128:(e + 1) * 128],
                         rhs=w_f[0:8, h * NH:(h + 1) * NH], start=True, stop=True)
            _inc(i, cT)
            v.wait_ge(cT.h, cT.n)
            i = v.tensor_copy(out=wb[:, h * NH:(h + 1) * NH], in_=ps[6][:, :])
            _inc(i, cV)
        wb_prev_v[0] = cV.n

    for e in range(E):
        for hc in range(NHC):
            sl = hc % 2
            pump(("w2", e, hc))
            c2 = cast_of[("w2", e, hc)]
            t.wait_ge(cS.h, max(cast_of[("w1", e, hc)], c2, ln_s))
            for h in range(2):
                pk1 = (e, hc - 1, h) if hc >= 1 else                     ((e - 1, NHC - 1, h) if e > 0 else None)
                if pk1 in h_ev:
                    t.wait_ge(cV.h, h_ev[pk1])
                if hc == 0 and e > 0:
                    t.wait_ge(cV.h, y_ev[(e - 1, NDC - 1)])
                if hc == 0 and e == 0:
                    t.wait_ge(cV.h, ln_vlast)
                lcol = h * NH
                for dc in range(NDC):
                    st = (dc == 0)
                    sp = (dc == NDC - 1)
                    t.matmul(out=ps[0 + h][:, :],
                             lhsT=w1bf[:, sl * 2048 + dc * 128: sl * 2048 + (dc + 1) * 128],
                             rhs=bx.ln[:, dc * 1024 + lcol: dc * 1024 + lcol + NH],
                             start=st, stop=sp)
                    i = t.matmul(out=ps[2 + h][:, :],
                                 lhsT=w2bf[:, sl * 2048 + dc * 128: sl * 2048 + (dc + 1) * 128],
                                 rhs=bx.ln[:, dc * 1024 + lcol: dc * 1024 + lcol + NH],
                                 start=st, stop=sp)
                _inc(i, cT)
                h_mm[(e, hc, h)] = cT.n
                if hc == 0 and h == 0:
                    emit_wb(e)
                s1 = bx.scr[:, h * NH:(h + 1) * NH]
                m1 = bx.scr[:, 1024 + h * NH: 1024 + (h + 1) * NH]
                sc.wait_ge(cT.h, h_mm[(e, hc, h)])
                i = sc.activation(out=s1, in_=ps[0 + h][:, :], func=AF.Silu)
                _inc(i, cS)
                v.wait_ge(cT.h, h_mm[(e, hc, h)])
                i = v.tensor_mul(out=m1, in0=ps[2 + h][:, :], in1=wb[:, h * NH:(h + 1) * NH])
                _inc(i, cV)
                v.wait_ge(cS.h, cS.n)
                v.wait_ge(cV.h, cV.n)
                i = v.tensor_mul(out=hbuf[:, hc * 1024 + h * NH: hc * 1024 + (h + 1) * NH],
                                 in0=s1, in1=m1)
                _inc(i, cV)
                h_ev[(e, hc, h)] = cV.n

        for dc in range(NDC):
            sl = dc % 2
            bp = (4, 5) if dc % 2 == 0 else (0, 1)
            pump(("w3", e, dc))
            t.wait_ge(cS.h, cast_of[("w3", e, dc)])
            t.wait_ge(cV.h, h_ev[(e, NHC - 1, 1)])
            pk1 = (e, dc - 2) if dc >= 2 else ((e - 1, NDC - 2 + dc) if e > 0 else None)
            if pk1 in y_ev:
                t.wait_ge(cV.h, y_ev[pk1])
            for hc in range(NHC):
                st = (hc == 0)
                sp = (hc == NHC - 1)
                for h in range(2):
                    i = t.matmul(out=ps[bp[h]][:, :],
                                 lhsT=w3bf[:, sl * 1408 + hc * 128: sl * 1408 + (hc + 1) * 128],
                                 rhs=hbuf[:, hc * 1024 + h * NH: hc * 1024 + (h + 1) * NH],
                                 start=st, stop=sp)
            _inc(i, cT)
            y_mm[(e, dc)] = cT.n
            for h in range(2):
                col = dc * 1024 + h * NH
                v.wait_ge(cT.h, y_mm[(e, dc)])
                i = v.tensor_add(out=bx.x_res[:, col:col + NH],
                                 in0=bx.x_res[:, col:col + NH], in1=ps[bp[h]][:, :])
                _inc(i, cV)
            y_ev[(e, dc)] = cV.n

    _join(bx, (nc.sync, nc.vector, nc.scalar, nc.gpsimd))
    es2.close()
    es.close()
    es0.close()


def emit_attn(bx, wq_in, wk_in, wv_in, wg_in, wp_in, upto=99):
    nc = bx.nc
    v, sc, g, t, sy = nc.vector, nc.scalar, nc.gpsimd, nc.tensor, nc.sync
    ps = bx.ps
    cT, cV, cS, cG = bx.cMM, bx.cDV, bx.cSC, bx.cGP
    cKV = bx.cKV

    es0 = ExitStack()
    bx.scr = es0.enter_context(nc.sbuf_tensor("at_scr", [128, 2048], F32, side="right"))
    bx.stg = es0.enter_context(nc.sbuf_tensor("at_stg", [128, 2 * 2048], F32, side="right"))
    bx.scrb2 = es0.enter_context(nc.sbuf_tensor("at_scrb2", [128, 2048], BF16, side="right"))

    ln_entry_cv = emit_ln(bx, with_rms=True, tag="atln")
    _join(bx, (nc.sync, nc.vector, nc.scalar, nc.gpsimd))

    es = ExitStack()
    es_w = ExitStack()
    wblk = es_w.enter_context(nc.sbuf_tensor("at_wblk", [128, 2 * 2048], BF16, side="right"))
    strm = Streamer(bx)
    blk_users = []

    def wsrc(w_in, oc):
        return w_in.rearrange("(dc p) m -> p dc m", p=128)[:, :, oc * 128:(oc + 1) * 128]

    def proj_mm_group(sl, psa, psb):
        for dc in range(NDC):
            st = (dc == 0)
            sp = (dc == NDC - 1)
            for h in range(2):
                pp = psa if h == 0 else psb
                i = t.matmul(out=pp[:, :],
                             lhsT=wblk[:, sl * 2048 + dc * 128: sl * 2048 + (dc + 1) * 128],
                             rhs=bx.ln[:, dc * 1024 + h * NH: dc * 1024 + (h + 1) * NH],
                             start=st, stop=sp)
        _inc(i, cT)
        blk_users.append(cT.n)
        return cT.n

    es_k = ExitStack()
    kfull = es_k.enter_context(nc.sbuf_tensor("at_kf", [128, NDC * 1024], BF16))
    k_ev = []
    cast_oc = {0: strm.stream(wsrc(wk_in, 0), wblk[:, 0:2048],
                              blk_users[-2] if len(blk_users) >= 2 else None)}
    for oc in range(NDC):
        sl = oc % 2
        pbank = (ps[0], ps[1]) if oc % 2 == 0 else (ps[2], ps[3])
        t.wait_ge(cS.h, cast_oc[oc])
        if oc == 0:
            t.wait_ge(cV.h, ln_entry_cv)
        if len(k_ev) >= 2:
            t.wait_ge(cS.h, k_ev[-2])
        proj_mm_group(sl, pbank[0], pbank[1])
        if oc + 1 < NDC:
            cast_oc[oc + 1] = strm.stream(wsrc(wk_in, oc + 1),
                                          wblk[:, ((oc + 1) % 2) * 2048:((oc + 1) % 2 + 1) * 2048],
                                          blk_users[-2] if len(blk_users) >= 2 else None)
        sc.wait_ge(cT.h, blk_users[-1])
        sc.activation(out=kfull[:, oc * 1024: oc * 1024 + 512], in_=pbank[0][:, :], func=AF.Silu)
        i = sc.activation(out=kfull[:, oc * 1024 + 512:(oc + 1) * 1024], in_=pbank[1][:, :],
                          func=AF.Silu)
        _inc(i, cS)
        k_ev.append(cS.n)
    g.wait_ge(cS.h, cS.n)
    g.dma_start(out=bx.shk.rearrange("s (dc p) tt -> s p dc tt", p=128)[bx.preg],
                in_=kfull[:, :].rearrange("p (dc tt) -> p dc tt", tt=1024)).then_inc(cKV.h, 16)
    cKV.n += 16
    g.wait_ge(cKV.h, cKV.n)
    g.dma_start(out=bx.shflag[bass.ts(bx.preg, 1)], in_=bx.nonce_sb[0:1, :]).then_inc(cKV.h, 16)
    cKV.n += 16
    v.wait_ge(cKV.h, cKV.n)
    es_k.close()

    es_v = ExitStack()
    vtm = es_v.enter_context(nc.sbuf_tensor("at_vtm", [128, NTC * 2048], BF16))
    silu_done = cS.n if k_ev else None
    tp_done = {}
    tp_ev = []
    cast_oc = {0: strm.stream(wsrc(wv_in, 0), wblk[:, 0:2048],
                              blk_users[-2] if len(blk_users) >= 2 else None)}
    silu_hist = []
    for oc in range(NDC):
        sl = oc % 2
        pbank = (ps[0], ps[1]) if oc % 2 == 0 else (ps[2], ps[3])
        t.wait_ge(cS.h, cast_oc[oc])
        if len(silu_hist) >= 2:
            t.wait_ge(cS.h, silu_hist[-2])
        elif silu_done is not None and not silu_hist:
            t.wait_ge(cS.h, silu_done)
        proj_mm_group(sl, pbank[0], pbank[1])
        if oc + 1 < NDC:
            cast_oc[oc + 1] = strm.stream(wsrc(wv_in, oc + 1),
                                          wblk[:, ((oc + 1) % 2) * 2048:((oc + 1) % 2 + 1) * 2048],
                                          blk_users[-2] if len(blk_users) >= 2 else None)
        vsl = oc % 2
        vfm = bx.scrb2[:, vsl * 1024:(vsl + 1) * 1024]
        sc.wait_ge(cT.h, blk_users[-1])
        if oc >= 2:
            sc.wait_ge(cT.h, tp_done[oc - 2])
        sc.activation(out=vfm[:, 0:512], in_=pbank[0][:, :], func=AF.Silu)
        i = sc.activation(out=vfm[:, 512:1024], in_=pbank[1][:, :], func=AF.Silu)
        _inc(i, cS)
        silu_hist.append(cS.n)
        t.wait_ge(cS.h, cS.n)
        if tp_ev:
            t.wait_ge(cV.h, tp_ev[-1])
        for tc in range(NTC):
            i = t.transpose(out=bx.pt[:, tc * 128:(tc + 1) * 128],
                            in_=vfm[:, tc * 128:(tc + 1) * 128], identity=bx.ident[:, :])
        _inc(i, cT)
        tp_done[oc] = cT.n
        v.wait_ge(cT.h, cT.n)
        for tc in range(NTC):
            i = v.tensor_copy(out=vtm[:, tc * 2048 + oc * 128: tc * 2048 + (oc + 1) * 128],
                              in_=bx.pt[:, tc * 128:(tc + 1) * 128])
        _inc(i, cV)
        tp_ev.append(cV.n)
    g.wait_ge(cV.h, cV.n)
    g.dma_start(out=bx.shv.rearrange("s (tc p) m -> s p tc m", p=128)[bx.preg],
                in_=vtm[:, :].rearrange("p (tc m) -> p tc m", m=2048)).then_inc(cKV.h, 16)
    cKV.n += 16
    g.wait_ge(cKV.h, cKV.n)
    g.dma_start(out=bx.shflag[bass.ts(2 + bx.preg, 1)],
                in_=bx.nonce_sb[0:1, :]).then_inc(cKV.h, 16)
    cKV.n += 16
    sc.wait_ge(cKV.h, cKV.n)
    es_v.close()

    es_q = ExitStack()
    qbuf = es_q.enter_context(nc.sbuf_tensor("at_q", [128, NDC * 1024], BF16))
    gbuf = es_q.enter_context(nc.sbuf_tensor("at_g", [128, NDC * 1024], BF16))
    pe = [cS.n]
    for (w_in, buf, func) in ((wq_in, qbuf, AF.Silu), (wg_in, gbuf, AF.Sigmoid)):
        cast_oc = {0: strm.stream(wsrc(w_in, 0), wblk[:, 0:2048],
                                  blk_users[-2] if len(blk_users) >= 2 else None)}
        for oc in range(NDC):
            sl = oc % 2
            pbank = (ps[0], ps[1]) if oc % 2 == 0 else (ps[2], ps[3])
            t.wait_ge(cS.h, cast_oc[oc])
            if len(pe) >= 2:
                t.wait_ge(cS.h, pe[-2])
            proj_mm_group(sl, pbank[0], pbank[1])
            if oc + 1 < NDC:
                cast_oc[oc + 1] = strm.stream(wsrc(w_in, oc + 1),
                                              wblk[:, ((oc + 1) % 2) * 2048:((oc + 1) % 2 + 1) * 2048],
                                              blk_users[-2] if len(blk_users) >= 2 else None)
            sc.wait_ge(cT.h, blk_users[-1])
            sc.activation(out=buf[:, oc * 1024: oc * 1024 + 512], in_=pbank[0][:, :], func=func)
            i = sc.activation(out=buf[:, oc * 1024 + 512: (oc + 1) * 1024], in_=pbank[1][:, :],
                              func=func)
            _inc(i, cS)
            pe.append(cS.n)
    proj_s = cS.n

    g.wait_ge(cS.h, proj_s)
    _emit_poll(bx, 1 - bx.preg)
    i = g.tensor_copy(out=bx.flrd_sb[0:1, :], in_=bx.nonce_sb[0:1, :])
    _inc(i, cG)
    poll_g = cG.n
    if upto <= 1:
        _barrier(bx)
        es_q.close()
        es_w.close()
        es0.close()
        es.close()
        return

    es_w.close()
    es0.close()
    es_a = ExitStack()
    stbf = es_a.enter_context(nc.sbuf_tensor("at_st", [128, 16 * 512], BF16))
    kch = es_a.enter_context(nc.sbuf_tensor("at_kch", [128, 2 * 2048], BF16))
    vch = es_a.enter_context(nc.sbuf_tensor("at_vch", [128, 4 * 2048], BF16))
    og = bx.ln

    sy.wait_ge(cG.h, poll_g)
    st_ev = {}
    o_mm = []
    og_ev = []
    smm = {}
    poll_gv = [None]
    for qg in range(2):
        nkc = 8 if qg == 0 else 16
        for kc in range(nkc):
            slot = kc % 2
            pos = kc // 2
            if kc >= 2:
                sy.wait_ge(cT.h, smm[(qg, kc - 2)])
            cks = bx.cKs[slot]
            _dma(bx, sy,
                 kch[:, slot * 2048:(slot + 1) * 2048].rearrange("p (dc m) -> p dc m", m=128),
                 bx.shk[kc % 2].rearrange("(dc p) tt -> p dc tt", p=128)[:, :, pos * 128:(pos + 1) * 128],
                 cks)
            pstart = max(0, kc // 2 - qg * 4)
            ncols = 512 - pstart * 128
            qcol0 = qg * 512 + pstart * 128
            t.wait_ge(cks.h, cks.n)
            t.wait_ge(cS.h, proj_s)
            if kc >= 2:
                t.wait_ge(cV.h, st_ev[(qg, kc - 2)])
            elif qg == 1:
                t.wait_ge(cV.h, st_ev[(0, 6 + kc)])
            for dc in range(NDC):
                i = t.matmul(out=ps[4 + (kc % 2)][:, pstart * 128:512],
                             lhsT=kch[:, slot * 2048 + dc * 128: slot * 2048 + (dc + 1) * 128],
                             rhs=qbuf[:, dc * 1024 + qcol0: dc * 1024 + qcol0 + ncols],
                             start=(dc == 0), stop=(dc == NDC - 1))
            _inc(i, cT)
            smm[(qg, kc)] = cT.n
            v.wait_ge(cT.h, cT.n)
            for pr in range(pstart, 4):
                p_abs = qg * 4 + pr
                rel = kc - 2 * p_abs
                dst = stbf[:, kc * 512 + pr * 128: kc * 512 + (pr + 1) * 128]
                src = ps[4 + (kc % 2)][:, pr * 128:(pr + 1) * 128]
                if rel <= -1:
                    i = v.tensor_copy(out=dst, in_=src)
                elif rel == 0:
                    i = v.tensor_tensor(out=dst, in0=src, in1=bx.masks[:, 0:128], op=ALU.mult)
                else:
                    i = v.tensor_tensor(out=dst, in0=src, in1=bx.masks[:, 128:256], op=ALU.mult)
            _inc(i, cV)
            st_ev[(qg, kc)] = cV.n
        if poll_gv[0] is None:
            _emit_poll(bx, 3 - bx.preg)
            i = g.tensor_copy(out=bx.flrd_sb[0:1, :], in_=bx.nonce_sb[0:1, :])
            _inc(i, cG)
            poll_gv[0] = cG.n
            sy.wait_ge(cG.h, poll_gv[0])
        for eg in range(4):
            for kc in range(nkc):
                slot = (eg * nkc + kc) % 4
                pos = kc // 2
                if len(o_mm) >= 4:
                    sy.wait_ge(cT.h, o_mm[-4])
                cvs = bx.cVs[slot]
                _dma(bx, sy, vch[:, slot * 2048:(slot + 1) * 2048],
                     bx.shv[kc % 2][pos * 128:(pos + 1) * 128, :], cvs)
                pstart = max(0, kc // 2 - qg * 4)
                t.wait_ge(cvs.h, cvs.n)
                t.wait_ge(cV.h, st_ev[(qg, kc)])
                if kc == 0 and og_ev:
                    t.wait_ge(cV.h, og_ev[-1])
                for ec in range(eg * 4, eg * 4 + 4):
                    i = t.matmul(out=ps[ec - eg * 4][:, pstart * 128:512],
                                 lhsT=vch[:, slot * 2048 + ec * 128: slot * 2048 + (ec + 1) * 128],
                                 rhs=stbf[:, kc * 512 + pstart * 128: (kc + 1) * 512],
                                 start=(kc == 0), stop=(kc == nkc - 1))
                _inc(i, cT)
                o_mm.append(cT.n)
            v.wait_ge(cT.h, cT.n)
            for ec in range(eg * 4, eg * 4 + 4):
                col = ec * 1024 + qg * 512
                i = v.tensor_tensor(out=og[:, col:col + 512], in0=ps[ec - eg * 4][:, :],
                                    in1=gbuf[:, col:col + 512], op=ALU.mult)
            _inc(i, cV)
            og_ev.append(cV.n)
    _join(bx, (nc.sync, nc.vector, nc.scalar, nc.gpsimd))
    es_a.close()
    es_q.close()
    if upto <= 2:
        es0_closed = True
        es.close()
        return

    es_o = ExitStack()
    bx.stg = es_o.enter_context(nc.sbuf_tensor("at_stg2", [128, 2 * 2048], F32))
    wblk = es_o.enter_context(nc.sbuf_tensor("at_wblk2", [128, 2 * 2048], BF16))
    out_ev = []
    cast_oc = {0: strm.stream(wsrc(wp_in, 0), wblk[:, 0:2048],
                              blk_users[-2] if len(blk_users) >= 2 else None)}
    for oc in range(NDC):
        sl = oc % 2
        pb0 = 0 if oc % 2 == 0 else 2
        t.wait_ge(cS.h, cast_oc[oc])
        if oc == 0 and og_ev:
            t.wait_ge(cV.h, og_ev[-1])
        if len(out_ev) >= 2:
            t.wait_ge(cV.h, out_ev[-2])
        for ec in range(NDC):
            st = (ec == 0)
            sp = (ec == NDC - 1)
            for h in range(2):
                i = t.matmul(out=ps[pb0 + h][:, :],
                             lhsT=wblk[:, sl * 2048 + ec * 128: sl * 2048 + (ec + 1) * 128],
                             rhs=og[:, ec * 1024 + h * NH: ec * 1024 + (h + 1) * NH],
                             start=st, stop=sp)
        _inc(i, cT)
        blk_users.append(cT.n)
        if oc + 1 < NDC:
            cast_oc[oc + 1] = strm.stream(wsrc(wp_in, oc + 1),
                                          wblk[:, ((oc + 1) % 2) * 2048:((oc + 1) % 2 + 1) * 2048],
                                          blk_users[-2] if len(blk_users) >= 2 else None)
        for h in range(2):
            col = oc * 1024 + h * NH
            v.wait_ge(cT.h, blk_users[-1])
            i = v.tensor_add(out=bx.x_res[:, col:col + NH],
                             in0=bx.x_res[:, col:col + NH], in1=ps[pb0 + h][:, :])
            _inc(i, cV)
        out_ev.append(cV.n)
    _join(bx, (nc.sync, nc.vector, nc.scalar, nc.gpsimd))
    es_o.close()
    es.close()


_POLL_N = [0]


def _emit_poll(bx, flag_idx):
    g = bx.nc.gpsimd
    cnt, fval, notready = bx.poll_regs
    _POLL_N[0] += 1
    g.reg_mov(notready, 1)
    with g.While(notready):
        g.dma_start(out=bx.flrd_sb[0:1, :],
                    in_=bx.shflag[bass.ts(flag_idx, 1)]).then_inc(bx.s_poll.h, 16)
        with g.register(f"tgt{_POLL_N[0]}") as tgt:
            g.reg_add(tgt, cnt, 16)
            g.wait_ge(bx.s_poll.h, tgt)
            g.reg_mov(cnt, tgt)
        g.load(fval, bx.flrd_sb[0:1, 0:1])
        g.reg_alu(notready, fval, bx.nreg, ALU.not_equal)


def _build(phases=("ffn1", "attn", "ffn2")):
    nc = bass.Bass("TRN2", target_bir_lowering=False, debug=False)
    bx = B()
    bx.nc = nc
    bx.ctrs = []
    es = ExitStack()
    bx.es = es

    x_in = nc.dram_tensor("x_in", [D, TOK], F32, kind="ExternalInput").ap()
    wq_in = nc.dram_tensor("wq", [D, D], F32, kind="ExternalInput").ap()
    wk_in = nc.dram_tensor("wk", [D, D], F32, kind="ExternalInput").ap()
    wv_in = nc.dram_tensor("wv", [D, D], F32, kind="ExternalInput").ap()
    wga_in = nc.dram_tensor("wga", [D, D], F32, kind="ExternalInput").ap()
    wp_in = nc.dram_tensor("wp", [D, D], F32, kind="ExternalInput").ap()
    wgf_in = nc.dram_tensor("wgf", [D, E], F32, kind="ExternalInput").ap()
    w1_in = nc.dram_tensor("w1", [E, D, HID], F32, kind="ExternalInput").ap()
    w2_in = nc.dram_tensor("w2", [E, D, HID], F32, kind="ExternalInput").ap()
    w3_in = nc.dram_tensor("w3", [E, HID, D], F32, kind="ExternalInput").ap()
    masks_in = nc.dram_tensor("masks", [2, 128, 128], F32, kind="ExternalInput").ap()
    sel_in = nc.dram_tensor("sel", [8, 1024], F32, kind="ExternalInput").ap()
    ident_in = nc.dram_tensor("ident", [128, 128], BF16, kind="ExternalInput").ap()
    consts_in = nc.dram_tensor("consts", [1, 8], F32, kind="ExternalInput").ap()
    meta_in = nc.dram_tensor("meta", [1, 8], I32, kind="ExternalInput").ap()
    nonce_in = nc.dram_tensor("nonce", [1, 32], I32, kind="ExternalInput").ap()
    out = nc.dram_tensor("out", [D, TOK], F32, kind="ExternalOutput").ap()

    bx.sel_in = sel_in
    bx.shk = nc.dram_tensor("shk", [2, D, TOK], BF16, kind="Internal",
                            addr_space="Shared").ap()
    bx.shv = nc.dram_tensor("shv", [2, TOK, D], BF16, kind="Internal",
                            addr_space="Shared").ap()
    bx.shflag = nc.dram_tensor("shflag", [4, 32], I32, kind="Internal",
                               addr_space="Shared").ap()

    bx.x_res = es.enter_context(nc.sbuf_tensor("x_res", [128, NDC * 1024], F32))
    bx.ln = es.enter_context(nc.sbuf_tensor("ln", [128, NDC * 1024], BF16))
    bx.masks = es.enter_context(nc.sbuf_tensor("sb_masks", [128, 2 * 128], F32))
    bx.ident = es.enter_context(nc.sbuf_tensor("sb_ident", [128, 128], BF16))
    bx.wgf_bf = es.enter_context(nc.sbuf_tensor("wgf_bf", [128, NDC * 8], BF16))
    bx.wgf_f = es.enter_context(nc.sbuf_tensor("wgf_f", [128, NDC * 8], F32))
    bx.consts = es.enter_context(nc.sbuf_tensor("sb_consts", [1, 8], F32))
    bx.meta_sb = es.enter_context(nc.sbuf_tensor("meta_sb", [1, 8], I32))
    bx.nonce_sb = es.enter_context(nc.sbuf_tensor("nonce_sb", [1, 32], I32))
    bx.flrd_sb = es.enter_context(nc.sbuf_tensor("flrd_sb", [1, 32], I32))
    bx.ones_lhs = es.enter_context(nc.sbuf_tensor("ones_lhs", [128, 1], BF16))
    bx.ones_f1 = es.enter_context(nc.sbuf_tensor("ones_f1", [1, 128], F32))

    bx.ps = [es.enter_context(nc.psum_tensor(f"ps{i}", [128, 512], F32)) for i in range(7)]
    bx.pt = es.enter_context(nc.psum_tensor("pt", [128, 1024], BF16))

    bx.cX = _mk_ctr(bx, "cX")
    bx.cWs = [_mk_ctr(bx, "cWa"), _mk_ctr(bx, "cWb")]
    bx.cKs = [_mk_ctr(bx, f"cK{i}") for i in range(2)]
    bx.cVs = [_mk_ctr(bx, f"cV{i}") for i in range(4)]
    bx.cKV = _mk_ctr(bx, "cKV")
    bx.cMM = _mk_ctr(bx, "cMM")
    bx.cDV = _mk_ctr(bx, "cDV")
    bx.cSC = _mk_ctr(bx, "cSC")
    bx.cGP = _mk_ctr(bx, "cGP")
    bx.cST = _mk_ctr(bx, "cST")
    bx.s_poll = Ctr(es.enter_context(nc.semaphore("s_poll")))

    sy, g = nc.sync, nc.gpsimd

    _dma(bx, sy, bx.x_res[:, :].rearrange("p (dc tt) -> p dc tt", tt=1024),
         x_in.rearrange("(dc p) tt -> p dc tt", p=128), bx.cX)
    _dma(bx, sy, bx.masks[:, :].rearrange("p (s m) -> p s m", s=2),
         masks_in.rearrange("s p m -> p s m"), bx.cX)
    _dma(bx, sy, bx.ident[:, :], ident_in, bx.cX)
    _dma(bx, sy, bx.consts[:, :], consts_in, bx.cX)
    _dma(bx, sy, bx.meta_sb[:, :], meta_in, bx.cX)
    _dma(bx, sy, bx.nonce_sb[:, :], nonce_in, bx.cX)
    _dma(bx, sy, bx.wgf_f[:, :].rearrange("p (dc e) -> p dc e", e=8),
         wgf_in.rearrange("(dc p) e -> p dc e", p=128), bx.cX)
    g.memset(bx.ones_lhs[:, :], 1.0 / 2048.0)
    g.memset(bx.ones_f1[:, :], 1.0)
    g.wait_ge(bx.cX.h, bx.cX.n)
    g.tensor_copy(out=bx.wgf_bf[:, :], in_=bx.wgf_f[:, :])
    bx.poll_regs = (g.alloc_register("pollcnt"), g.alloc_register("pollval"),
                    g.alloc_register("pollnr"))
    g.reg_mov(bx.poll_regs[0], 0)
    preg_raw = g.alloc_register("parity")
    g.load(preg_raw, bx.meta_sb[0:1, 0:1])
    bx.preg = g.snap(preg_raw, min_val=0, max_val=1)
    bx.nreg = g.alloc_register("nonce_r")
    g.load(bx.nreg, bx.nonce_sb[0:1, 0:1])
    i = g.tensor_copy(out=bx.flrd_sb[0:1, :], in_=bx.nonce_sb[0:1, :])
    _inc(i, bx.cGP)
    _barrier(bx)

    if "ffn1" in phases:
        emit_ffn(bx, w1_in, w2_in, w3_in, "f1")
    if "attn1" in phases:
        emit_attn(bx, wq_in, wk_in, wv_in, wga_in, wp_in, upto=1)
    if "attn2" in phases:
        emit_attn(bx, wq_in, wk_in, wv_in, wga_in, wp_in, upto=2)
    if "attn" in phases:
        emit_attn(bx, wq_in, wk_in, wv_in, wga_in, wp_in)
    if "ffn2" in phases:
        emit_ffn(bx, w1_in, w2_in, w3_in, "f2")

    _barrier(bx)
    _dma(bx, sy, out.rearrange("(dc p) tt -> p dc tt", p=128),
         bx.x_res[:, :].rearrange("p (dc tt) -> p dc tt", tt=1024), bx.cST)
    for eng in _engines(nc):
        eng.wait_ge(bx.cST.h, bx.cST.n)
    es.close()
    return nc


_NC_CACHE = {}


def _get_nc(phases):
    key = tuple(phases)
    if key not in _NC_CACHE:
        _NC_CACHE[key] = _build(key)
    return _NC_CACHE[key]


def _host_inputs(inputs):
    x = np.asarray(inputs["L0_attn_x"], np.float32)
    wq = np.ascontiguousarray(np.asarray(inputs["L0_attn_Wq"], np.float32).T)
    wk = np.ascontiguousarray(np.asarray(inputs["L0_attn_Wk"], np.float32).T)
    wv = np.ascontiguousarray(np.asarray(inputs["L0_attn_Wv"], np.float32).T)
    wga = np.ascontiguousarray(np.asarray(inputs["L0_attn_Wg"], np.float32).T)
    wp = np.ascontiguousarray(np.asarray(inputs["L0_attn_Wp"], np.float32).T)
    wgf = np.ascontiguousarray(np.asarray(inputs["L0_ffn_Wg"], np.float32).T)
    w1 = np.ascontiguousarray(np.transpose(np.asarray(inputs["L0_ffn_W1"], np.float32), (0, 2, 1)))
    w2 = np.ascontiguousarray(np.transpose(np.asarray(inputs["L0_ffn_W2"], np.float32), (0, 2, 1)))
    w3 = np.ascontiguousarray(np.transpose(np.asarray(inputs["L0_ffn_W3"], np.float32), (0, 2, 1)))
    import ml_dtypes
    ident = np.eye(128, dtype=ml_dtypes.bfloat16)
    sel = np.zeros((8, 1024), np.float32)
    for e in range(E):
        sel[e, e * 128:(e + 1) * 128] = 1.0
    consts = np.zeros((1, 8), np.float32)
    consts[0, 0] = EPS_LN
    consts[0, 1] = EPS_RMS
    nonce = np.int32((time.time_ns() // 7) & 0x7FFFFFFF) | np.int32(1)
    tri = np.triu(np.ones((128, 128), np.float32))
    in_maps = []
    rows_per_core = []
    for core in range(8):
        b = core // 2
        parity = core % 2
        chunks = [2 * p + parity for p in range(NTC)]
        rows = np.concatenate([np.arange(c * 128, (c + 1) * 128) for c in chunks])
        rows_per_core.append((b, rows))
        x_fm = np.ascontiguousarray(x[b][rows].T)
        if parity == 0:
            masks = np.stack([tri, np.zeros((128, 128), np.float32)])
        else:
            masks = np.stack([np.ones((128, 128), np.float32), tri])
        meta = np.zeros((1, 8), np.int32)
        meta[0, 0] = parity
        in_maps.append({
            "x_in": x_fm, "wq": wq, "wk": wk, "wv": wv, "wga": wga, "wp": wp,
            "wgf": wgf, "w1": w1, "w2": w2, "w3": w3,
            "masks": masks, "sel": sel, "ident": ident, "consts": consts,
            "meta": meta, "nonce": np.full((1, 32), nonce, np.int32),
        })
    return in_maps, rows_per_core


def _run(inputs, phases):
    nc = _get_nc(phases)
    in_maps, rows_per_core = _host_inputs(inputs)
    res = run_bass_kernel_spmd(nc, in_maps, core_ids=list(range(8)))
    out = np.empty((4, 2048, 2048), np.float32)
    for core in range(8):
        b, rows = rows_per_core[core]
        out[b][rows] = np.asarray(res.results[core]["out"]).T
    return out


def kernel(**inputs):
    return _run(inputs, ("ffn1", "attn", "ffn2"))


# version 07
# speedup vs baseline: 1.8780x; 1.8780x over previous
import time
import numpy as np
from contextlib import ExitStack

import concourse.bass as bass
import concourse.mybir as mybir
from concourse.bass_utils import run_bass_kernel_spmd

F32 = mybir.dt.float32
BF16 = mybir.dt.bfloat16
I32 = mybir.dt.int32
AF = mybir.ActivationFunctionType
ALU = mybir.AluOpType
AX = mybir.AxisListType

D = 2048
NDC = 16
TOK = 1024
NTC = 8
E = 8
HID = 1408
NHC = 11
NH = 512
EPS_LN = 1e-5
EPS_RMS = float(np.finfo(np.float32).eps)


class Ctr:
    __slots__ = ("h", "n")

    def __init__(self, h):
        self.h = h
        self.n = 0


class B:
    pass


def _mk_ctr(bx, name):
    c = Ctr(bx.es.enter_context(bx.nc.semaphore(name)))
    bx.ctrs.append(c)
    return c


def _dma(bx, eng, out, in_, ctr):
    eng.dma_start(out=out, in_=in_).then_inc(ctr.h, 16)
    ctr.n += 16
    return ctr.n


def _inc(instr, ctr, k=1):
    instr.then_inc(ctr.h, k)
    ctr.n += k
    return ctr.n


def _engines(nc):
    return (nc.sync, nc.tensor, nc.vector, nc.scalar, nc.gpsimd)


def _barrier(bx):
    for eng in _engines(bx.nc):
        for c in bx.ctrs:
            if c.n:
                eng.wait_ge(c.h, c.n)


def _join(bx, engs):
    for eng in engs:
        for c in bx.ctrs:
            if c.n:
                eng.wait_ge(c.h, c.n)


def emit_ln(bx, with_rms, tag, gate=False):
    nc = bx.nc
    v, sc, g, t = nc.vector, nc.scalar, nc.gpsimd, nc.tensor
    ps = bx.ps
    cT, cV, cS, cG = bx.cMM, bx.cDV, bx.cSC, bx.cGP

    es = ExitStack()
    if gate:
        lnf = es.enter_context(nc.sbuf_tensor(f"{tag}_lnf", [128, 2048], F32))
    scrb = es.enter_context(nc.sbuf_tensor(f"{tag}_xbf", [128, 2048], BF16))
    sqb = es.enter_context(nc.sbuf_tensor(f"{tag}_sqb", [128, 2048], BF16))
    stats = es.enter_context(nc.sbuf_tensor(f"{tag}_stats", [1, 4096], F32))
    A_b = es.enter_context(nc.sbuf_tensor(f"{tag}_Ab", [128, 1024], F32))
    B_b = es.enter_context(nc.sbuf_tensor(f"{tag}_Bb", [128, 1024], F32))
    mean_r = stats[:, 0:1024]
    var_r = stats[:, 1024:2048]
    s_r = stats[:, 2048:3072]
    t_r = stats[:, 3072:4096]

    entry_cV = cV.n
    entry_cS = cS.n
    cast_done = []
    sq_done = []
    mm_done = []
    for dc in range(NDC):
        sl = dc % 2
        xb = scrb[:, sl * 1024:(sl + 1) * 1024]
        sq = sqb[:, sl * 1024:(sl + 1) * 1024]
        if dc >= 2:
            g.wait_ge(cT.h, mm_done[dc - 2])
        i = g.tensor_copy(out=xb, in_=bx.x_res[:, dc * 1024:(dc + 1) * 1024])
        _inc(i, cG)
        cast_done.append(cG.n)
        sc.wait_ge(cG.h, cG.n)
        i = sc.activation(out=sq, in_=xb, func=AF.Square)
        _inc(i, cS)
        sq_done.append(cS.n)
        t.wait_ge(cG.h, cast_done[dc])
        t.wait_ge(cS.h, sq_done[dc])
        if dc == 0:
            if entry_cV:
                t.wait_ge(cV.h, entry_cV)
            if entry_cS:
                t.wait_ge(cS.h, entry_cS)
        st = (dc == 0)
        sp = (dc == NDC - 1)
        for h in range(2):
            t.matmul(out=ps[0 + h][0:1, :], lhsT=bx.ones_lhs[:, 0:1],
                     rhs=xb[:, h * NH:(h + 1) * NH], start=st, stop=sp)
            i = t.matmul(out=ps[2 + h][0:1, :], lhsT=bx.ones_lhs[:, 0:1],
                         rhs=sq[:, h * NH:(h + 1) * NH], start=st, stop=sp)
        _inc(i, cT)
        mm_done.append(cT.n)

    v.wait_ge(cT.h, cT.n)
    v.tensor_copy(out=mean_r[:, 0:512], in_=ps[0][0:1, :])
    v.tensor_copy(out=mean_r[:, 512:1024], in_=ps[1][0:1, :])
    v.tensor_copy(out=var_r[:, 0:512], in_=ps[2][0:1, :])
    i = v.tensor_copy(out=var_r[:, 512:1024], in_=ps[3][0:1, :])
    _inc(i, cV)

    def vstep(fn):
        v.wait_ge(cV.h, cV.n)
        _inc(fn(), cV)

    def scstep(fn):
        sc.wait_ge(cV.h, cV.n)
        _inc(fn(), cS)
        v.wait_ge(cS.h, cS.n)

    vstep(lambda: v.tensor_mul(out=t_r, in0=mean_r, in1=mean_r))
    vstep(lambda: v.tensor_sub(out=var_r, in0=var_r, in1=t_r))
    scstep(lambda: sc.activation(out=t_r, in_=var_r, func=AF.Sqrt,
                                 bias=bx.consts[0:1, 0:1]))
    vstep(lambda: v.reciprocal(out=s_r, in_=t_r))
    if with_rms:
        vstep(lambda: v.tensor_mul(out=t_r, in0=s_r, in1=s_r))
        vstep(lambda: v.tensor_mul(out=t_r, in0=var_r, in1=t_r))
        scstep(lambda: sc.activation(out=t_r, in_=t_r, func=AF.Sqrt,
                                     bias=bx.consts[0:1, 1:2]))
        vstep(lambda: v.reciprocal(out=t_r, in_=t_r))
        vstep(lambda: v.tensor_mul(out=s_r, in0=s_r, in1=t_r))
    vstep(lambda: v.tensor_mul(out=t_r, in0=mean_r, in1=s_r))

    t.wait_ge(cV.h, cV.n)
    for h in range(2):
        t.matmul(out=ps[0 + h][:, :], lhsT=bx.ones_f1[0:1, :],
                 rhs=s_r[0:1, h * NH:(h + 1) * NH], start=True, stop=True)
        i = t.matmul(out=ps[2 + h][:, :], lhsT=bx.ones_f1[0:1, :],
                     rhs=t_r[0:1, h * NH:(h + 1) * NH], start=True, stop=True)
    _inc(i, cT)
    v.wait_ge(cT.h, cT.n)
    v.tensor_copy(out=A_b[:, 0:512], in_=ps[0][:, :])
    v.tensor_copy(out=A_b[:, 512:1024], in_=ps[1][:, :])
    v.tensor_copy(out=B_b[:, 0:512], in_=ps[2][:, :])
    i = v.tensor_copy(out=B_b[:, 512:1024], in_=ps[3][:, :])
    _inc(i, cV)
    bcast_v = cV.n

    apply_v = []
    lnf_users = []
    for dc in range(NDC):
        sl = dc % 2
        tslot = bx.scr[:, sl * 1024:(sl + 1) * 1024]
        g.wait_ge(cV.h, bcast_v)
        if dc >= 2:
            g.wait_ge(cV.h, apply_v[dc - 2])
        i = g.tensor_mul(out=tslot, in0=bx.x_res[:, dc * 1024:(dc + 1) * 1024], in1=A_b[:, :])
        _inc(i, cG)
        v.wait_ge(cG.h, cG.n)
        if not gate:
            i = v.tensor_sub(out=bx.ln[:, dc * 1024:(dc + 1) * 1024], in0=tslot, in1=B_b[:, :])
            _inc(i, cV)
            apply_v.append(cV.n)
        else:
            lslot = lnf[:, sl * 1024:(sl + 1) * 1024]
            if dc >= 2:
                v.wait_ge(cT.h, lnf_users[dc - 2][0])
                v.wait_ge(cS.h, lnf_users[dc - 2][1])
            i = v.tensor_sub(out=lslot, in0=tslot, in1=B_b[:, :])
            _inc(i, cV)
            apply_v.append(cV.n)
            sc.wait_ge(cV.h, cV.n)
            i = sc.activation(out=bx.ln[:, dc * 1024:(dc + 1) * 1024], in_=lslot, func=AF.Copy)
            _inc(i, cS)
            t.wait_ge(cV.h, cV.n)
            for h in range(2):
                i = t.matmul(out=ps[4 + h][0:8, :], lhsT=bx.wgf_f[:, dc * 8:(dc + 1) * 8],
                             rhs=lslot[:, h * NH:(h + 1) * NH],
                             start=(dc == 0), stop=(dc == NDC - 1))
            _inc(i, cT)
            lnf_users.append((cT.n, cS.n))
    es.close()
    if gate:
        return cS.n, cT.n
    return cV.n


class Streamer:

    def __init__(self, bx):
        self.bx = bx
        self.idx = 0
        self.cast_hist = []

    def stream(self, src_ap, dst_bf, free_at_cT):
        bx = self.bx
        nc = bx.nc
        sy, sc = nc.sync, nc.scalar
        slot = self.idx % 2
        self.idx += 1
        F = dst_bf.shape[-1]
        stg = bx.stg[:, slot * 2048: slot * 2048 + F]
        if len(self.cast_hist) >= 2:
            sy.wait_ge(bx.cSC.h, self.cast_hist[-2])
        cws = bx.cWs[slot]
        _dma(bx, sy, stg.rearrange("p (c m) -> p c m", m=128), src_ap, cws)
        sc.wait_ge(cws.h, cws.n)
        if free_at_cT is not None:
            sc.wait_ge(bx.cMM.h, free_at_cT)
        i = sc.activation(out=dst_bf, in_=stg, func=AF.Copy)
        _inc(i, bx.cSC)
        self.cast_hist.append(bx.cSC.n)
        return bx.cSC.n


def emit_ffn(bx, w1_in, w2_in, w3_in, tag):
    nc = bx.nc
    v, sc, g, t = nc.vector, nc.scalar, nc.gpsimd, nc.tensor
    ps = bx.ps
    cT, cV, cS, cG = bx.cMM, bx.cDV, bx.cSC, bx.cGP

    es0 = ExitStack()
    bx.scr = es0.enter_context(nc.sbuf_tensor(f"{tag}_scr", [128, 2048], F32))
    bx.stg = es0.enter_context(nc.sbuf_tensor(f"{tag}_stg", [128, 2 * 2048], F32))

    ln_s, gate_t = emit_ln(bx, with_rms=False, tag=tag + "ln", gate=True)
    ln_vlast = cV.n
    _join(bx, (nc.sync, nc.vector, nc.scalar, nc.gpsimd))

    es = ExitStack()
    sc_f = es.enter_context(nc.sbuf_tensor(f"{tag}_scf", [32, 1024], F32))
    tv = es.enter_context(nc.sbuf_tensor(f"{tag}_tv", [32, 1024], F32))
    eq1 = es.enter_context(nc.sbuf_tensor(f"{tag}_eq1", [32, 256], F32))
    eq2 = es.enter_context(nc.sbuf_tensor(f"{tag}_eq2", [32, 256], F32))
    scm = es.enter_context(nc.sbuf_tensor(f"{tag}_scm", [32, 256], F32))
    mx1 = es.enter_context(nc.sbuf_tensor(f"{tag}_mx1", [32, 32], F32))
    mx2 = es.enter_context(nc.sbuf_tensor(f"{tag}_mx2", [32, 32], F32))
    p1h = es.enter_context(nc.sbuf_tensor(f"{tag}_p1h", [32, 32], F32))
    p2h = es.enter_context(nc.sbuf_tensor(f"{tag}_p2h", [32, 32], F32))
    gw = es.enter_context(nc.sbuf_tensor(f"{tag}_gw", [32, 1024], F32))
    w_f = es.enter_context(nc.sbuf_tensor(f"{tag}_wf", [32, 1024], F32))

    def vstep(fn):
        v.wait_ge(cV.h, cV.n)
        _inc(fn(), cV)

    v.wait_ge(cT.h, gate_t)
    _inc(v.memset(sc_f[:, :], 0.0), cV)
    _inc(v.memset(gw[:, :], 0.0), cV)
    vstep(lambda: v.tensor_copy(out=sc_f[0:8, 0:512], in_=ps[4][0:8, :]))
    vstep(lambda: v.tensor_copy(out=sc_f[0:8, 512:1024], in_=ps[5][0:8, :]))
    vstep(lambda: v.transpose(out=tv[:, :], in_=sc_f[:, :]))
    tvv = tv[:, :].rearrange("p (cb e) -> p cb e", e=32)[:, :, 0:8]
    eq1v = eq1[:, :].rearrange("p (cb e) -> p cb e", e=8)
    eq2v = eq2[:, :].rearrange("p (cb e) -> p cb e", e=8)
    scmv = scm[:, :].rearrange("p (cb e) -> p cb e", e=8)
    gwv = gw[:, :].rearrange("p (cb e) -> p cb e", e=32)[:, :, 0:8]
    mx1v = mx1[:, :].rearrange("p (cb o) -> p cb o", o=1)
    mx2v = mx2[:, :].rearrange("p (cb o) -> p cb o", o=1)
    p1v = p1h[:, :].rearrange("p (cb o) -> p cb o", o=1)
    p2v = p2h[:, :].rearrange("p (cb o) -> p cb o", o=1)
    vstep(lambda: v.reduce_max(out=mx1v, in_=tvv, axis=AX.X))
    vstep(lambda: v.tensor_tensor(out=eq1v, in0=tvv, in1=mx1v.broadcast_to([32, 32, 8]),
                                  op=ALU.is_equal))
    vstep(lambda: v.scalar_tensor_tensor(out=scmv, in0=eq1v, scalar=-1e30, in1=tvv,
                                         op0=ALU.mult, op1=ALU.add))
    vstep(lambda: v.reduce_max(out=mx2v, in_=scmv, axis=AX.X))
    vstep(lambda: v.tensor_tensor(out=eq2v, in0=scmv, in1=mx2v.broadcast_to([32, 32, 8]),
                                  op=ALU.is_equal))
    vstep(lambda: v.tensor_sub(out=p2h[:, :], in0=mx1[:, :], in1=mx2[:, :]))
    sc.wait_ge(cV.h, cV.n)
    _inc(sc.activation(out=p1h[:, :], in_=p2h[:, :], func=AF.Sigmoid), bx.cSC)
    v.wait_ge(bx.cSC.h, bx.cSC.n)
    _inc(v.tensor_scalar(out=p1h[:, :], in0=p1h[:, :], scalar1=0.5, scalar2=None,
                         op0=ALU.mult), cV)
    vstep(lambda: v.tensor_scalar(out=p2h[:, :], in0=p1h[:, :], scalar1=-1.0, scalar2=0.5,
                                  op0=ALU.mult, op1=ALU.add))
    vstep(lambda: v.tensor_tensor(out=gwv, in0=eq1v, in1=p1v.broadcast_to([32, 32, 8]),
                                  op=ALU.mult))
    vstep(lambda: v.tensor_tensor(out=eq2v, in0=eq2v, in1=p2v.broadcast_to([32, 32, 8]),
                                  op=ALU.mult))
    vstep(lambda: v.tensor_tensor(out=gwv, in0=gwv, in1=eq2v, op=ALU.add))
    vstep(lambda: v.transpose(out=w_f[:, :], in_=gw[:, :]))
    wf_v = cV.n

    es2 = ExitStack()
    hbuf = es2.enter_context(nc.sbuf_tensor(f"{tag}_h", [128, NHC * 1024], BF16))
    sel_sb = es2.enter_context(nc.sbuf_tensor(f"{tag}_sel", [8, 1024], F32))
    _dma(bx, nc.sync, sel_sb[:, :], bx.sel_in, bx.cX)
    w1bf = es2.enter_context(nc.sbuf_tensor(f"{tag}_w1b", [128, 2 * 2048], BF16))
    w2bf = es2.enter_context(nc.sbuf_tensor(f"{tag}_w2b", [128, 2 * 2048], BF16))
    w3bf = es2.enter_context(nc.sbuf_tensor(f"{tag}_w3b", [128, 2 * 1408], BF16))
    wb = es2.enter_context(nc.sbuf_tensor(f"{tag}_wb", [128, 1024], F32))

    strm = Streamer(bx)
    h_mm = {}
    y_mm = {}
    h_ev = {}
    y_ev = {}

    def w1_src(e, hc):
        return w1_in[e].rearrange("(dc p) m -> p dc m", p=128)[:, :, hc * 128:(hc + 1) * 128]

    def w2_src(e, hc):
        return w2_in[e].rearrange("(dc p) m -> p dc m", p=128)[:, :, hc * 128:(hc + 1) * 128]

    def w3_src(e, dc):
        return w3_in[e].rearrange("(hc p) m -> p hc m", p=128)[:, :, dc * 128:(dc + 1) * 128]

    sched = []
    for e in range(E):
        for hc in range(NHC):
            sched.append(("w1", e, hc))
            sched.append(("w2", e, hc))
        for dc in range(NDC):
            sched.append(("w3", e, dc))
    cast_of = {}
    sidx = {ent: i for i, ent in enumerate(sched)}
    ptr = [0]

    def _freeat(ent):
        kind, e, j = ent
        if kind in ("w1", "w2"):
            pk = (e, j - 2) if j >= 2 else ((e - 1, NHC - 2 + j) if e > 0 else None)
            return (pk is None, h_mm.get(pk + (1,)) if pk is not None else None)
        pk = (e, j - 2) if j >= 2 else ((e - 1, NDC - 2 + j) if e > 0 else None)
        return (pk is None, y_mm.get(pk))

    def _emit(ent):
        kind, e, j = ent
        sl = j % 2
        fresh, fa = _freeat(ent)
        if kind == "w1":
            cast_of[ent] = strm.stream(w1_src(e, j), w1bf[:, sl * 2048:(sl + 1) * 2048], fa)
        elif kind == "w2":
            cast_of[ent] = strm.stream(w2_src(e, j), w2bf[:, sl * 2048:(sl + 1) * 2048], fa)
        else:
            cast_of[ent] = strm.stream(w3_src(e, j), w3bf[:, sl * 1408:(sl + 1) * 1408], fa)

    def pump(ent, ahead=3):
        target = sidx[ent] + ahead
        while ptr[0] < len(sched) and (sched[ptr[0]] in (ent,) or ptr[0] <= target):
            nxt = sched[ptr[0]]
            if nxt != ent and sidx[nxt] > sidx[ent]:
                fresh, fa = _freeat(nxt)
                if not fresh and fa is None:
                    break
            _emit(nxt)
            ptr[0] += 1
            if nxt == ent:
                if ptr[0] > target:
                    break

    wb_prev_v = [None]

    def emit_wb(e):
        t.wait_ge(cV.h, wf_v)
        t.wait_ge(bx.cX.h, bx.cX.n)
        if wb_prev_v[0] is not None:
            t.wait_ge(cV.h, wb_prev_v[0])
        for h in range(2):
            if h == 1:
                t.wait_ge(cV.h, cV.n)
            i = t.matmul(out=ps[6][:, :], lhsT=sel_sb[:, e * 128:(e + 1) * 128],
                         rhs=w_f[0:8, h * NH:(h + 1) * NH], start=True, stop=True)
            _inc(i, cT)
            v.wait_ge(cT.h, cT.n)
            i = v.tensor_copy(out=wb[:, h * NH:(h + 1) * NH], in_=ps[6][:, :])
            _inc(i, cV)
        wb_prev_v[0] = cV.n

    for e in range(E):
        for hc in range(NHC):
            sl = hc % 2
            pump(("w2", e, hc))
            c2 = cast_of[("w2", e, hc)]
            t.wait_ge(cS.h, max(cast_of[("w1", e, hc)], c2, ln_s))
            for h in range(2):
                pk1 = (e, hc - 1, h) if hc >= 1 else                     ((e - 1, NHC - 1, h) if e > 0 else None)
                if pk1 in h_ev:
                    t.wait_ge(cV.h, h_ev[pk1])
                if hc == 0 and e > 0:
                    t.wait_ge(cV.h, y_ev[(e - 1, NDC - 1)])
                if hc == 0 and e == 0:
                    t.wait_ge(cV.h, ln_vlast)
                lcol = h * NH
                for dc in range(NDC):
                    st = (dc == 0)
                    sp = (dc == NDC - 1)
                    t.matmul(out=ps[0 + h][:, :],
                             lhsT=w1bf[:, sl * 2048 + dc * 128: sl * 2048 + (dc + 1) * 128],
                             rhs=bx.ln[:, dc * 1024 + lcol: dc * 1024 + lcol + NH],
                             start=st, stop=sp)
                    i = t.matmul(out=ps[2 + h][:, :],
                                 lhsT=w2bf[:, sl * 2048 + dc * 128: sl * 2048 + (dc + 1) * 128],
                                 rhs=bx.ln[:, dc * 1024 + lcol: dc * 1024 + lcol + NH],
                                 start=st, stop=sp)
                _inc(i, cT)
                h_mm[(e, hc, h)] = cT.n
                if hc == 0 and h == 0:
                    emit_wb(e)
                s1 = bx.scr[:, h * NH:(h + 1) * NH]
                m1 = bx.scr[:, 1024 + h * NH: 1024 + (h + 1) * NH]
                sc.wait_ge(cT.h, h_mm[(e, hc, h)])
                i = sc.activation(out=s1, in_=ps[0 + h][:, :], func=AF.Silu)
                _inc(i, cS)
                v.wait_ge(cT.h, h_mm[(e, hc, h)])
                i = v.tensor_mul(out=m1, in0=ps[2 + h][:, :], in1=wb[:, h * NH:(h + 1) * NH])
                _inc(i, cV)
                v.wait_ge(cS.h, cS.n)
                v.wait_ge(cV.h, cV.n)
                i = v.tensor_mul(out=hbuf[:, hc * 1024 + h * NH: hc * 1024 + (h + 1) * NH],
                                 in0=s1, in1=m1)
                _inc(i, cV)
                h_ev[(e, hc, h)] = cV.n

        for dc in range(NDC):
            sl = dc % 2
            bp = (4, 5) if dc % 2 == 0 else (0, 1)
            pump(("w3", e, dc))
            t.wait_ge(cS.h, cast_of[("w3", e, dc)])
            t.wait_ge(cV.h, h_ev[(e, NHC - 1, 1)])
            pk1 = (e, dc - 2) if dc >= 2 else ((e - 1, NDC - 2 + dc) if e > 0 else None)
            if pk1 in y_ev:
                t.wait_ge(cV.h, y_ev[pk1])
            for hc in range(NHC):
                st = (hc == 0)
                sp = (hc == NHC - 1)
                for h in range(2):
                    i = t.matmul(out=ps[bp[h]][:, :],
                                 lhsT=w3bf[:, sl * 1408 + hc * 128: sl * 1408 + (hc + 1) * 128],
                                 rhs=hbuf[:, hc * 1024 + h * NH: hc * 1024 + (h + 1) * NH],
                                 start=st, stop=sp)
            _inc(i, cT)
            y_mm[(e, dc)] = cT.n
            for h in range(2):
                col = dc * 1024 + h * NH
                v.wait_ge(cT.h, y_mm[(e, dc)])
                i = v.tensor_add(out=bx.x_res[:, col:col + NH],
                                 in0=bx.x_res[:, col:col + NH], in1=ps[bp[h]][:, :])
                _inc(i, cV)
            y_ev[(e, dc)] = cV.n

    _join(bx, (nc.sync, nc.vector, nc.scalar, nc.gpsimd))
    es2.close()
    es.close()
    es0.close()


def emit_attn(bx, wq_in, wk_in, wv_in, wg_in, wp_in, upto=99):
    nc = bx.nc
    v, sc, g, t, sy = nc.vector, nc.scalar, nc.gpsimd, nc.tensor, nc.sync
    ps = bx.ps
    cT, cV, cS, cG = bx.cMM, bx.cDV, bx.cSC, bx.cGP
    cKV = bx.cKV

    es0 = ExitStack()
    bx.scr = es0.enter_context(nc.sbuf_tensor("at_scr", [128, 2048], F32, side="right"))
    bx.stg = es0.enter_context(nc.sbuf_tensor("at_stg", [128, 2 * 2048], F32, side="right"))
    bx.scrb2 = es0.enter_context(nc.sbuf_tensor("at_scrb2", [128, 2048], BF16, side="right"))

    ln_entry_cv = emit_ln(bx, with_rms=True, tag="atln")
    _join(bx, (nc.sync, nc.vector, nc.scalar, nc.gpsimd))

    es = ExitStack()
    es_w = ExitStack()
    wblk = es_w.enter_context(nc.sbuf_tensor("at_wblk", [128, 2 * 2048], BF16, side="right"))
    strm = Streamer(bx)
    blk_users = []

    def wsrc(w_in, oc):
        return w_in.rearrange("(dc p) m -> p dc m", p=128)[:, :, oc * 128:(oc + 1) * 128]

    def proj_mm_group(sl, psa, psb):
        for dc in range(NDC):
            st = (dc == 0)
            sp = (dc == NDC - 1)
            for h in range(2):
                pp = psa if h == 0 else psb
                i = t.matmul(out=pp[:, :],
                             lhsT=wblk[:, sl * 2048 + dc * 128: sl * 2048 + (dc + 1) * 128],
                             rhs=bx.ln[:, dc * 1024 + h * NH: dc * 1024 + (h + 1) * NH],
                             start=st, stop=sp)
        _inc(i, cT)
        blk_users.append(cT.n)
        return cT.n

    es_k = ExitStack()
    kfull = es_k.enter_context(nc.sbuf_tensor("at_kf", [128, NDC * 1024], BF16))
    k_ev = []
    cast_oc = {0: strm.stream(wsrc(wk_in, 0), wblk[:, 0:2048],
                              blk_users[-2] if len(blk_users) >= 2 else None)}
    for oc in range(NDC):
        sl = oc % 2
        pbank = (ps[0], ps[1]) if oc % 2 == 0 else (ps[2], ps[3])
        t.wait_ge(cS.h, cast_oc[oc])
        if oc == 0:
            t.wait_ge(cV.h, ln_entry_cv)
        if len(k_ev) >= 2:
            t.wait_ge(cS.h, k_ev[-2])
        proj_mm_group(sl, pbank[0], pbank[1])
        if oc + 1 < NDC:
            cast_oc[oc + 1] = strm.stream(wsrc(wk_in, oc + 1),
                                          wblk[:, ((oc + 1) % 2) * 2048:((oc + 1) % 2 + 1) * 2048],
                                          blk_users[-2] if len(blk_users) >= 2 else None)
        sc.wait_ge(cT.h, blk_users[-1])
        sc.activation(out=kfull[:, oc * 1024: oc * 1024 + 512], in_=pbank[0][:, :], func=AF.Silu)
        i = sc.activation(out=kfull[:, oc * 1024 + 512:(oc + 1) * 1024], in_=pbank[1][:, :],
                          func=AF.Silu)
        _inc(i, cS)
        k_ev.append(cS.n)
    g.wait_ge(cS.h, cS.n)
    g.dma_start(out=bx.shk.rearrange("s (dc p) tt -> s p dc tt", p=128)[bx.preg],
                in_=kfull[:, :].rearrange("p (dc tt) -> p dc tt", tt=1024)).then_inc(cKV.h, 16)
    cKV.n += 16
    g.wait_ge(cKV.h, cKV.n)
    g.dma_start(out=bx.shflag[bass.ts(bx.preg, 1)], in_=bx.nonce_sb[0:1, :]).then_inc(cKV.h, 16)
    cKV.n += 16
    v.wait_ge(cKV.h, cKV.n)
    es_k.close()

    es_v = ExitStack()
    vtm = es_v.enter_context(nc.sbuf_tensor("at_vtm", [128, NTC * 2048], BF16))
    silu_done = cS.n if k_ev else None
    tp_done = {}
    tp_ev = []
    cast_oc = {0: strm.stream(wsrc(wv_in, 0), wblk[:, 0:2048],
                              blk_users[-2] if len(blk_users) >= 2 else None)}
    silu_hist = []
    for oc in range(NDC):
        sl = oc % 2
        pbank = (ps[0], ps[1]) if oc % 2 == 0 else (ps[2], ps[3])
        t.wait_ge(cS.h, cast_oc[oc])
        if len(silu_hist) >= 2:
            t.wait_ge(cS.h, silu_hist[-2][0])
        elif silu_done is not None and not silu_hist:
            t.wait_ge(cS.h, silu_done)
        proj_mm_group(sl, pbank[0], pbank[1])
        if oc + 1 < NDC:
            cast_oc[oc + 1] = strm.stream(wsrc(wv_in, oc + 1),
                                          wblk[:, ((oc + 1) % 2) * 2048:((oc + 1) % 2 + 1) * 2048],
                                          blk_users[-2] if len(blk_users) >= 2 else None)
        vsl = oc % 2
        vfm = bx.scrb2[:, vsl * 1024:(vsl + 1) * 1024]

        def emit_tp(j):
            jsl = j % 2
            jfm = bx.scrb2[:, jsl * 1024:(jsl + 1) * 1024]
            t.wait_ge(cS.h, silu_hist[j][1])
            if tp_ev:
                t.wait_ge(cV.h, tp_ev[-1])
            for tc in range(NTC):
                i = t.transpose(out=bx.pt[:, tc * 128:(tc + 1) * 128],
                                in_=jfm[:, tc * 128:(tc + 1) * 128], identity=bx.ident[:, :])
            _inc(i, cT)
            tp_done[j] = cT.n
            v.wait_ge(cT.h, cT.n)
            for tc in range(NTC):
                i = v.tensor_copy(out=vtm[:, tc * 2048 + j * 128: tc * 2048 + (j + 1) * 128],
                                  in_=bx.pt[:, tc * 128:(tc + 1) * 128])
            _inc(i, cV)
            tp_ev.append(cV.n)

        if oc >= 1:
            emit_tp(oc - 1)
        sc.wait_ge(cT.h, blk_users[-1])
        if oc >= 2:
            sc.wait_ge(cT.h, tp_done[oc - 2])
        sc.activation(out=vfm[:, 0:512], in_=pbank[0][:, :], func=AF.Silu)
        i = sc.activation(out=vfm[:, 512:1024], in_=pbank[1][:, :], func=AF.Silu)
        _inc(i, cS)
        silu_hist.append((cS.n, cS.n))
    emit_tp(NDC - 1)
    g.wait_ge(cV.h, cV.n)
    g.dma_start(out=bx.shv.rearrange("s (tc p) m -> s p tc m", p=128)[bx.preg],
                in_=vtm[:, :].rearrange("p (tc m) -> p tc m", m=2048)).then_inc(cKV.h, 16)
    cKV.n += 16
    g.wait_ge(cKV.h, cKV.n)
    g.dma_start(out=bx.shflag[bass.ts(2 + bx.preg, 1)],
                in_=bx.nonce_sb[0:1, :]).then_inc(cKV.h, 16)
    cKV.n += 16
    sc.wait_ge(cKV.h, cKV.n)
    es_v.close()

    es_q = ExitStack()
    qbuf = es_q.enter_context(nc.sbuf_tensor("at_q", [128, NDC * 1024], BF16))
    gbuf = es_q.enter_context(nc.sbuf_tensor("at_g", [128, NDC * 1024], BF16))
    pe = [cS.n]
    for (w_in, buf, func) in ((wq_in, qbuf, AF.Silu), (wg_in, gbuf, AF.Sigmoid)):
        cast_oc = {0: strm.stream(wsrc(w_in, 0), wblk[:, 0:2048],
                                  blk_users[-2] if len(blk_users) >= 2 else None)}
        for oc in range(NDC):
            sl = oc % 2
            pbank = (ps[0], ps[1]) if oc % 2 == 0 else (ps[2], ps[3])
            t.wait_ge(cS.h, cast_oc[oc])
            if len(pe) >= 2:
                t.wait_ge(cS.h, pe[-2])
            proj_mm_group(sl, pbank[0], pbank[1])
            if oc + 1 < NDC:
                cast_oc[oc + 1] = strm.stream(wsrc(w_in, oc + 1),
                                              wblk[:, ((oc + 1) % 2) * 2048:((oc + 1) % 2 + 1) * 2048],
                                              blk_users[-2] if len(blk_users) >= 2 else None)
            sc.wait_ge(cT.h, blk_users[-1])
            sc.activation(out=buf[:, oc * 1024: oc * 1024 + 512], in_=pbank[0][:, :], func=func)
            i = sc.activation(out=buf[:, oc * 1024 + 512: (oc + 1) * 1024], in_=pbank[1][:, :],
                              func=func)
            _inc(i, cS)
            pe.append(cS.n)
    proj_s = cS.n

    g.wait_ge(cS.h, proj_s)
    _emit_poll(bx, 1 - bx.preg)
    i = g.tensor_copy(out=bx.flrd_sb[0:1, :], in_=bx.nonce_sb[0:1, :])
    _inc(i, cG)
    poll_g = cG.n
    if upto <= 1:
        _barrier(bx)
        es_q.close()
        es_w.close()
        es0.close()
        es.close()
        return

    es_w.close()
    es0.close()
    es_a = ExitStack()
    stbf = es_a.enter_context(nc.sbuf_tensor("at_st", [128, 16 * 512], BF16))
    kch = es_a.enter_context(nc.sbuf_tensor("at_kch", [128, 2 * 2048], BF16))
    vch = es_a.enter_context(nc.sbuf_tensor("at_vch", [128, 8 * 512], BF16))
    og = bx.ln

    sy.wait_ge(cG.h, poll_g)
    st_ev = {}
    o_mm = []
    og_ev = []
    smm = {}
    poll_gv = [None]
    for qg in range(2):
        nkc = 8 if qg == 0 else 16
        for kc in range(nkc):
            slot = kc % 2
            pos = kc // 2
            if kc >= 2:
                sy.wait_ge(cT.h, smm[(qg, kc - 2)])
            cks = bx.cKs[slot]
            _dma(bx, sy,
                 kch[:, slot * 2048:(slot + 1) * 2048].rearrange("p (dc m) -> p dc m", m=128),
                 bx.shk[kc % 2].rearrange("(dc p) tt -> p dc tt", p=128)[:, :, pos * 128:(pos + 1) * 128],
                 cks)
            pstart = max(0, kc // 2 - qg * 4)
            ncols = 512 - pstart * 128
            qcol0 = qg * 512 + pstart * 128
            t.wait_ge(cks.h, cks.n)
            t.wait_ge(cS.h, proj_s)
            if kc >= 2:
                t.wait_ge(cV.h, st_ev[(qg, kc - 2)])
            elif qg == 1:
                t.wait_ge(cV.h, st_ev[(0, 6 + kc)])
            for dc in range(NDC):
                i = t.matmul(out=ps[4 + (kc % 2)][:, pstart * 128:512],
                             lhsT=kch[:, slot * 2048 + dc * 128: slot * 2048 + (dc + 1) * 128],
                             rhs=qbuf[:, dc * 1024 + qcol0: dc * 1024 + qcol0 + ncols],
                             start=(dc == 0), stop=(dc == NDC - 1))
            _inc(i, cT)
            smm[(qg, kc)] = cT.n
            v.wait_ge(cT.h, cT.n)
            for pr in range(pstart, 4):
                p_abs = qg * 4 + pr
                rel = kc - 2 * p_abs
                dst = stbf[:, kc * 512 + pr * 128: kc * 512 + (pr + 1) * 128]
                src = ps[4 + (kc % 2)][:, pr * 128:(pr + 1) * 128]
                if rel <= -1:
                    i = v.tensor_copy(out=dst, in_=src)
                elif rel == 0:
                    i = v.tensor_tensor(out=dst, in0=src, in1=bx.masks[:, 0:128], op=ALU.mult)
                else:
                    i = v.tensor_tensor(out=dst, in0=src, in1=bx.masks[:, 128:256], op=ALU.mult)
            _inc(i, cV)
            st_ev[(qg, kc)] = cV.n
        if poll_gv[0] is None:
            _emit_poll(bx, 3 - bx.preg)
            i = g.tensor_copy(out=bx.flrd_sb[0:1, :], in_=bx.nonce_sb[0:1, :])
            _inc(i, cG)
            poll_gv[0] = cG.n
            sy.wait_ge(cG.h, poll_gv[0])
        for eg in range(4):
            for kc in range(nkc):
                slot = (eg * nkc + kc) % 8
                pos = kc // 2
                if len(o_mm) >= 8:
                    sy.wait_ge(cT.h, o_mm[-8])
                cvs = bx.cVs[slot]
                _dma(bx, sy, vch[:, slot * 512:(slot + 1) * 512],
                     bx.shv[kc % 2][pos * 128:(pos + 1) * 128, eg * 512:(eg + 1) * 512], cvs)
                pstart = max(0, kc // 2 - qg * 4)
                t.wait_ge(cvs.h, cvs.n)
                t.wait_ge(cV.h, st_ev[(qg, kc)])
                if kc == 0 and og_ev:
                    t.wait_ge(cV.h, og_ev[-1])
                for ec in range(eg * 4, eg * 4 + 4):
                    i = t.matmul(out=ps[ec - eg * 4][:, pstart * 128:512],
                                 lhsT=vch[:, slot * 512 + (ec - eg * 4) * 128:
                                          slot * 512 + (ec - eg * 4 + 1) * 128],
                                 rhs=stbf[:, kc * 512 + pstart * 128: (kc + 1) * 512],
                                 start=(kc == 0), stop=(kc == nkc - 1))
                _inc(i, cT)
                o_mm.append(cT.n)
            v.wait_ge(cT.h, cT.n)
            for ec in range(eg * 4, eg * 4 + 4):
                col = ec * 1024 + qg * 512
                i = v.tensor_tensor(out=og[:, col:col + 512], in0=ps[ec - eg * 4][:, :],
                                    in1=gbuf[:, col:col + 512], op=ALU.mult)
            _inc(i, cV)
            og_ev.append(cV.n)
    _join(bx, (nc.sync, nc.vector, nc.scalar, nc.gpsimd))
    es_a.close()
    es_q.close()
    if upto <= 2:
        es0_closed = True
        es.close()
        return

    es_o = ExitStack()
    bx.stg = es_o.enter_context(nc.sbuf_tensor("at_stg2", [128, 2 * 2048], F32))
    wblk = es_o.enter_context(nc.sbuf_tensor("at_wblk2", [128, 2 * 2048], BF16))
    out_ev = []
    cast_oc = {0: strm.stream(wsrc(wp_in, 0), wblk[:, 0:2048],
                              blk_users[-2] if len(blk_users) >= 2 else None)}
    for oc in range(NDC):
        sl = oc % 2
        pb0 = 0 if oc % 2 == 0 else 2
        t.wait_ge(cS.h, cast_oc[oc])
        if oc == 0 and og_ev:
            t.wait_ge(cV.h, og_ev[-1])
        if len(out_ev) >= 2:
            t.wait_ge(cV.h, out_ev[-2])
        for ec in range(NDC):
            st = (ec == 0)
            sp = (ec == NDC - 1)
            for h in range(2):
                i = t.matmul(out=ps[pb0 + h][:, :],
                             lhsT=wblk[:, sl * 2048 + ec * 128: sl * 2048 + (ec + 1) * 128],
                             rhs=og[:, ec * 1024 + h * NH: ec * 1024 + (h + 1) * NH],
                             start=st, stop=sp)
        _inc(i, cT)
        blk_users.append(cT.n)
        if oc + 1 < NDC:
            cast_oc[oc + 1] = strm.stream(wsrc(wp_in, oc + 1),
                                          wblk[:, ((oc + 1) % 2) * 2048:((oc + 1) % 2 + 1) * 2048],
                                          blk_users[-2] if len(blk_users) >= 2 else None)
        for h in range(2):
            col = oc * 1024 + h * NH
            v.wait_ge(cT.h, blk_users[-1])
            i = v.tensor_add(out=bx.x_res[:, col:col + NH],
                             in0=bx.x_res[:, col:col + NH], in1=ps[pb0 + h][:, :])
            _inc(i, cV)
        out_ev.append(cV.n)
    _join(bx, (nc.sync, nc.vector, nc.scalar, nc.gpsimd))
    es_o.close()
    es.close()


_POLL_N = [0]


def _emit_poll(bx, flag_idx):
    g = bx.nc.gpsimd
    cnt, fval, notready = bx.poll_regs
    _POLL_N[0] += 1
    g.reg_mov(notready, 1)
    with g.While(notready):
        g.dma_start(out=bx.flrd_sb[0:1, :],
                    in_=bx.shflag[bass.ts(flag_idx, 1)]).then_inc(bx.s_poll.h, 16)
        with g.register(f"tgt{_POLL_N[0]}") as tgt:
            g.reg_add(tgt, cnt, 16)
            g.wait_ge(bx.s_poll.h, tgt)
            g.reg_mov(cnt, tgt)
        g.load(fval, bx.flrd_sb[0:1, 0:1])
        g.reg_alu(notready, fval, bx.nreg, ALU.not_equal)


def _build(phases=("ffn1", "attn", "ffn2")):
    nc = bass.Bass("TRN2", target_bir_lowering=False, debug=False)
    bx = B()
    bx.nc = nc
    bx.ctrs = []
    es = ExitStack()
    bx.es = es

    x_in = nc.dram_tensor("x_in", [D, TOK], F32, kind="ExternalInput").ap()
    wq_in = nc.dram_tensor("wq", [D, D], F32, kind="ExternalInput").ap()
    wk_in = nc.dram_tensor("wk", [D, D], F32, kind="ExternalInput").ap()
    wv_in = nc.dram_tensor("wv", [D, D], F32, kind="ExternalInput").ap()
    wga_in = nc.dram_tensor("wga", [D, D], F32, kind="ExternalInput").ap()
    wp_in = nc.dram_tensor("wp", [D, D], F32, kind="ExternalInput").ap()
    wgf_in = nc.dram_tensor("wgf", [D, E], F32, kind="ExternalInput").ap()
    w1_in = nc.dram_tensor("w1", [E, D, HID], F32, kind="ExternalInput").ap()
    w2_in = nc.dram_tensor("w2", [E, D, HID], F32, kind="ExternalInput").ap()
    w3_in = nc.dram_tensor("w3", [E, HID, D], F32, kind="ExternalInput").ap()
    masks_in = nc.dram_tensor("masks", [2, 128, 128], F32, kind="ExternalInput").ap()
    sel_in = nc.dram_tensor("sel", [8, 1024], F32, kind="ExternalInput").ap()
    ident_in = nc.dram_tensor("ident", [128, 128], BF16, kind="ExternalInput").ap()
    consts_in = nc.dram_tensor("consts", [1, 8], F32, kind="ExternalInput").ap()
    meta_in = nc.dram_tensor("meta", [1, 8], I32, kind="ExternalInput").ap()
    nonce_in = nc.dram_tensor("nonce", [1, 32], I32, kind="ExternalInput").ap()
    out = nc.dram_tensor("out", [D, TOK], F32, kind="ExternalOutput").ap()

    bx.sel_in = sel_in
    bx.shk = nc.dram_tensor("shk", [2, D, TOK], BF16, kind="Internal",
                            addr_space="Shared").ap()
    bx.shv = nc.dram_tensor("shv", [2, TOK, D], BF16, kind="Internal",
                            addr_space="Shared").ap()
    bx.shflag = nc.dram_tensor("shflag", [4, 32], I32, kind="Internal",
                               addr_space="Shared").ap()

    bx.x_res = es.enter_context(nc.sbuf_tensor("x_res", [128, NDC * 1024], F32))
    bx.ln = es.enter_context(nc.sbuf_tensor("ln", [128, NDC * 1024], BF16))
    bx.masks = es.enter_context(nc.sbuf_tensor("sb_masks", [128, 2 * 128], F32))
    bx.ident = es.enter_context(nc.sbuf_tensor("sb_ident", [128, 128], BF16))
    bx.wgf_bf = es.enter_context(nc.sbuf_tensor("wgf_bf", [128, NDC * 8], BF16))
    bx.wgf_f = es.enter_context(nc.sbuf_tensor("wgf_f", [128, NDC * 8], F32))
    bx.consts = es.enter_context(nc.sbuf_tensor("sb_consts", [1, 8], F32))
    bx.meta_sb = es.enter_context(nc.sbuf_tensor("meta_sb", [1, 8], I32))
    bx.nonce_sb = es.enter_context(nc.sbuf_tensor("nonce_sb", [1, 32], I32))
    bx.flrd_sb = es.enter_context(nc.sbuf_tensor("flrd_sb", [1, 32], I32))
    bx.ones_lhs = es.enter_context(nc.sbuf_tensor("ones_lhs", [128, 1], BF16))
    bx.ones_f1 = es.enter_context(nc.sbuf_tensor("ones_f1", [1, 128], F32))

    bx.ps = [es.enter_context(nc.psum_tensor(f"ps{i}", [128, 512], F32)) for i in range(7)]
    bx.pt = es.enter_context(nc.psum_tensor("pt", [128, 1024], BF16))

    bx.cX = _mk_ctr(bx, "cX")
    bx.cWs = [_mk_ctr(bx, "cWa"), _mk_ctr(bx, "cWb")]
    bx.cKs = [_mk_ctr(bx, f"cK{i}") for i in range(2)]
    bx.cVs = [_mk_ctr(bx, f"cV{i}") for i in range(8)]
    bx.cKV = _mk_ctr(bx, "cKV")
    bx.cMM = _mk_ctr(bx, "cMM")
    bx.cDV = _mk_ctr(bx, "cDV")
    bx.cSC = _mk_ctr(bx, "cSC")
    bx.cGP = _mk_ctr(bx, "cGP")
    bx.cST = _mk_ctr(bx, "cST")
    bx.s_poll = Ctr(es.enter_context(nc.semaphore("s_poll")))

    sy, g = nc.sync, nc.gpsimd

    _dma(bx, sy, bx.x_res[:, :].rearrange("p (dc tt) -> p dc tt", tt=1024),
         x_in.rearrange("(dc p) tt -> p dc tt", p=128), bx.cX)
    _dma(bx, sy, bx.masks[:, :].rearrange("p (s m) -> p s m", s=2),
         masks_in.rearrange("s p m -> p s m"), bx.cX)
    _dma(bx, sy, bx.ident[:, :], ident_in, bx.cX)
    _dma(bx, sy, bx.consts[:, :], consts_in, bx.cX)
    _dma(bx, sy, bx.meta_sb[:, :], meta_in, bx.cX)
    _dma(bx, sy, bx.nonce_sb[:, :], nonce_in, bx.cX)
    _dma(bx, sy, bx.wgf_f[:, :].rearrange("p (dc e) -> p dc e", e=8),
         wgf_in.rearrange("(dc p) e -> p dc e", p=128), bx.cX)
    g.memset(bx.ones_lhs[:, :], 1.0 / 2048.0)
    g.memset(bx.ones_f1[:, :], 1.0)
    g.wait_ge(bx.cX.h, bx.cX.n)
    g.tensor_copy(out=bx.wgf_bf[:, :], in_=bx.wgf_f[:, :])
    bx.poll_regs = (g.alloc_register("pollcnt"), g.alloc_register("pollval"),
                    g.alloc_register("pollnr"))
    g.reg_mov(bx.poll_regs[0], 0)
    preg_raw = g.alloc_register("parity")
    g.load(preg_raw, bx.meta_sb[0:1, 0:1])
    bx.preg = g.snap(preg_raw, min_val=0, max_val=1)
    bx.nreg = g.alloc_register("nonce_r")
    g.load(bx.nreg, bx.nonce_sb[0:1, 0:1])
    i = g.tensor_copy(out=bx.flrd_sb[0:1, :], in_=bx.nonce_sb[0:1, :])
    _inc(i, bx.cGP)
    _barrier(bx)

    if "ffn1" in phases:
        emit_ffn(bx, w1_in, w2_in, w3_in, "f1")
    if "attn1" in phases:
        emit_attn(bx, wq_in, wk_in, wv_in, wga_in, wp_in, upto=1)
    if "attn2" in phases:
        emit_attn(bx, wq_in, wk_in, wv_in, wga_in, wp_in, upto=2)
    if "attn" in phases:
        emit_attn(bx, wq_in, wk_in, wv_in, wga_in, wp_in)
    if "ffn2" in phases:
        emit_ffn(bx, w1_in, w2_in, w3_in, "f2")

    _barrier(bx)
    _dma(bx, sy, out.rearrange("(dc p) tt -> p dc tt", p=128),
         bx.x_res[:, :].rearrange("p (dc tt) -> p dc tt", tt=1024), bx.cST)
    for eng in _engines(nc):
        eng.wait_ge(bx.cST.h, bx.cST.n)
    es.close()
    return nc


_NC_CACHE = {}


def _get_nc(phases):
    key = tuple(phases)
    if key not in _NC_CACHE:
        _NC_CACHE[key] = _build(key)
    return _NC_CACHE[key]


def _host_inputs(inputs):
    x = np.asarray(inputs["L0_attn_x"], np.float32)
    wq = np.ascontiguousarray(np.asarray(inputs["L0_attn_Wq"], np.float32).T)
    wk = np.ascontiguousarray(np.asarray(inputs["L0_attn_Wk"], np.float32).T)
    wv = np.ascontiguousarray(np.asarray(inputs["L0_attn_Wv"], np.float32).T)
    wga = np.ascontiguousarray(np.asarray(inputs["L0_attn_Wg"], np.float32).T)
    wp = np.ascontiguousarray(np.asarray(inputs["L0_attn_Wp"], np.float32).T)
    wgf = np.ascontiguousarray(np.asarray(inputs["L0_ffn_Wg"], np.float32).T)
    w1 = np.ascontiguousarray(np.transpose(np.asarray(inputs["L0_ffn_W1"], np.float32), (0, 2, 1)))
    w2 = np.ascontiguousarray(np.transpose(np.asarray(inputs["L0_ffn_W2"], np.float32), (0, 2, 1)))
    w3 = np.ascontiguousarray(np.transpose(np.asarray(inputs["L0_ffn_W3"], np.float32), (0, 2, 1)))
    import ml_dtypes
    ident = np.eye(128, dtype=ml_dtypes.bfloat16)
    sel = np.zeros((8, 1024), np.float32)
    for e in range(E):
        sel[e, e * 128:(e + 1) * 128] = 1.0
    consts = np.zeros((1, 8), np.float32)
    consts[0, 0] = EPS_LN
    consts[0, 1] = EPS_RMS
    nonce = np.int32((time.time_ns() // 7) & 0x7FFFFFFF) | np.int32(1)
    tri = np.triu(np.ones((128, 128), np.float32))
    in_maps = []
    rows_per_core = []
    for core in range(8):
        b = core // 2
        parity = core % 2
        chunks = [2 * p + parity for p in range(NTC)]
        rows = np.concatenate([np.arange(c * 128, (c + 1) * 128) for c in chunks])
        rows_per_core.append((b, rows))
        x_fm = np.ascontiguousarray(x[b][rows].T)
        if parity == 0:
            masks = np.stack([tri, np.zeros((128, 128), np.float32)])
        else:
            masks = np.stack([np.ones((128, 128), np.float32), tri])
        meta = np.zeros((1, 8), np.int32)
        meta[0, 0] = parity
        in_maps.append({
            "x_in": x_fm, "wq": wq, "wk": wk, "wv": wv, "wga": wga, "wp": wp,
            "wgf": wgf, "w1": w1, "w2": w2, "w3": w3,
            "masks": masks, "sel": sel, "ident": ident, "consts": consts,
            "meta": meta, "nonce": np.full((1, 32), nonce, np.int32),
        })
    return in_maps, rows_per_core


def _run(inputs, phases):
    nc = _get_nc(phases)
    in_maps, rows_per_core = _host_inputs(inputs)
    res = run_bass_kernel_spmd(nc, in_maps, core_ids=list(range(8)))
    out = np.empty((4, 2048, 2048), np.float32)
    for core in range(8):
        b, rows = rows_per_core[core]
        out[b][rows] = np.asarray(res.results[core]["out"]).T
    return out


def kernel(**inputs):
    return _run(inputs, ("ffn1", "attn", "ffn2"))
